# version 23
# baseline (speedup 1.0000x reference)
import contextlib
import numpy as np
import ml_dtypes
import concourse.bass as bass
import concourse.mybir as mybir
from concourse.bass_utils import run_bass_kernel_spmd

F32 = mybir.dt.float32
BF16 = mybir.dt.bfloat16
FP8 = mybir.dt.float8e4
I16 = mybir.dt.int16
AF = mybir.ActivationFunctionType
ALU = mybir.AluOpType
AX = mybir.AxisListType

D = 2048
FF = 8192
H = 16
ND = D // 128
NFS = FF // 512
NFC = FF // 128
EPS = 1e-5
RT2 = float(128.0 ** -0.5)

ENGINES = ("sync", "tensor", "vector", "scalar", "gpsimd")


class Ct:
    def __init__(self, h):
        self.h = h
        self.n = 0

    def inc(self, k=1):
        self.n += k
        return self.n


class MK:

    def __init__(self, T, parts):
        self.T = T
        self.NT = T // 128
        self.NQS = T // 512
        self.parts = parts
        self.nc = bass.Bass("TRN2", target_bir_lowering=False)
        self.plan = {e: [] for e in ENGINES}
        self.dsems = {}
        self.dbg = False
        self.n_heads = H
        self.nqs_cap = 99
        self.fm_off = False
        self.vt_off = False
        self.proj_off = False

    def q(self, eng, fname, *args, incs=(), **kw):
        for ct, amt in incs:
            ct.inc(amt)
        self.plan[eng].append(("call", fname, args, kw, tuple(incs)))

    def w(self, eng, ct, val):
        if val > 0:
            self.plan[eng].append(("wait", ct, val))

    def dr(self, eng):
        self.plan[eng].append(("drain",))

    def dsem(self, name):
        if name not in self.dsems:
            self.dsems[name] = Ct(self._sem_stack.enter_context(
                self.nc.semaphore("d_" + name)))
        return self.dsems[name]

    def dma(self, queue, out, in_, name):
        ct = self.dsem(name)
        self.q(queue, "dma_start", out=out, in_=in_, incs=[(ct, 16)])
        return ct.n

    def wdma(self, eng, name):
        ct = self.dsem(name)
        self.w(eng, ct, ct.n)

    def barrier(self):
        engs = ("tensor", "vector", "scalar", "gpsimd", "sync")
        for e in engs:
            self.q(e, "nop", incs=[(self.cB[e], 1)])
        cts = [self.cT, self.cV, self.cS, self.cG, self.cY]
        vals = ([(self.cB[e], self.cB[e].n) for e in engs]
                + [(c, c.n) for c in cts]
                + [(c, c.n) for c in self.dsems.values()])
        for e in engs:
            for c, v in vals:
                self.w(e, c, v)

    def sb(self, stack, name, shape, dt):
        return stack.enter_context(self.nc.sbuf_tensor(name, list(shape), dt))

    def build(self):
        nc = self.nc
        T, NT, NQS = self.T, self.NT, self.NQS

        def din(name, shape, dt=F32):
            return nc.dram_tensor(name, list(shape), dt, kind="ExternalInput").ap()

        x_in = din("x", (T, D))
        w1t = din("w1t", (D, FF))
        w2t = din("w2t", (FF, D))
        wt = {nm: din("w%st" % nm, (D, D)) for nm in ("q", "k", "v", "o")}
        b1r = din("b1r", (16, 512))
        b2r = din("b2r", (4, 512))
        gFr = din("gFr", (16, 512))
        bFr = din("bFr", (16, 512))
        gamc = din("gamc", (128, 1))
        betc = din("betc", (128, 1))
        identf = din("identf", (128, 128))
        identb = din("identb", (128, 128), BF16)
        onesf = din("onesf", (1, 128))
        onescf = din("onescf", (128, 1))
        onescb = din("onescb", (128, 1), BF16)
        mask4 = din("mask4", (128, 896), BF16)

        y_out = nc.dram_tensor("y", [T, D], F32, kind="ExternalOutput").ap()

        def scr(name, shape, dt):
            return nc.dram_tensor(name, list(shape), dt, kind="Internal").ap()

        s_w1q = scr("s_w1q", (D, FF), FP8)
        s_w2q = scr("s_w2q", (128, NFC * D), FP8)
        s_g = scr("s_g", (T, FF), F32)
        s_hq = scr("s_hq", (NT, 128, NFC * 128), BF16)
        s_x1 = scr("s_x1", (T, D), F32)
        s_x2 = scr("s_x2", (T, D), F32)
        s_o = scr("s_o", (NT, 128, H * 2 * 128), BF16)
        s_wh = {nm: scr("s_w%sh" % nm, (H, 2, ND, 128, 128), BF16)
                for nm in ("q", "k", "v")}
        s_woh = scr("s_woh", (4, 2, ND, 128, 512), BF16)
        s_dbg = (nc.dram_tensor("dbg", [16, 128, 512], F32, kind="ExternalOutput").ap()
                 if self.dbg else None)

        sem_stack = contextlib.ExitStack()
        self._sem_stack = sem_stack
        mk_sem = lambda nm: sem_stack.enter_context(nc.semaphore(nm))
        self.dL = Ct(mk_sem("dL"))
        self.dS = Ct(mk_sem("dS"))
        self.cT = Ct(mk_sem("cT"))
        self.cV = Ct(mk_sem("cV"))
        self.cS = Ct(mk_sem("cS"))
        self.cG = Ct(mk_sem("cG"))
        self.cY = Ct(mk_sem("cY"))
        self.cB = {e: Ct(mk_sem("cB_" + e))
                   for e in ("tensor", "vector", "scalar", "gpsimd", "sync")}
        dL, dS, cT, cV, cS = self.dL, self.dS, self.cT, self.cV, self.cS

        P = contextlib.ExitStack()
        self._P = P
        k_idf = self.sb(P, "k_idf", (128, 128), F32)
        k_idb = self.sb(P, "k_idb", (128, 128), BF16)
        k_1all = self.sb(P, "k_1all", (128, 128), F32)
        k_1cf = self.sb(P, "k_1cf", (128, 1), F32)
        k_1cb = self.sb(P, "k_1cb", (128, 1), BF16)
        k_gamc = self.sb(P, "k_gamc", (128, 1), F32)
        k_betc = self.sb(P, "k_betc", (128, 1), F32)
        st_t = self.sb(P, "st_t", (128, 16), F32)
        st_bn = self.sb(P, "st_bn", (128, 24), F32)
        st_sc = self.sb(P, "st_sc", (128, 4), F32)
        k127i = self.sb(P, "k127i", (128, 1), F32)
        kRT2 = self.sb(P, "kRT2", (128, 1), F32)

        psF = P.enter_context(nc.psum_tensor("psF", [128, 3584], F32))
        psH = P.enter_context(nc.psum_tensor("psH", [128, 1024], BF16))

        def bank(i, wdt=512):
            return psF[:, 512 * i:512 * i + wdt]

        self.bank_free = {}

        def mm_bank_wait(b):
            if b in self.bank_free:
                c, v = self.bank_free[b]
                self.w("tensor", c, v)

        def pe_bcast(psum_ap, row_ap, ones=None):
            self.q("tensor", "matmul", psum_ap,
                   k_1all[0:1, :] if ones is None else ones, row_ap,
                   start=True, stop=True, incs=[(cT, 1)])
            return cT.n

        def ln_stats_tile(xb, mu_dst, rstd_dst):
            for c in range(4):
                self.q("vector", "bn_stats", out=st_bn[:, 6 * c:6 * (c + 1)],
                       in_=xb[:, 512 * c:512 * (c + 1)])
            self.dr("vector")
            self.q("vector", "bn_aggr", out=st_t[:, 0:2], in_=st_bn[:, 0:24])
            self.dr("vector")
            self.q("vector", "tensor_copy", out=mu_dst, in_=st_t[:, 0:1])
            self.dr("vector")
            self.q("vector", "tensor_scalar", out=st_t[:, 2:3], in0=st_t[:, 1:2],
                   scalar1=EPS, scalar2=None, op0=ALU.add)
            self.dr("vector")
            self.q("vector", "reciprocal", out=st_t[:, 3:4], in_=st_t[:, 2:3],
                   incs=[(cV, 1)])
            self.dr("vector")
            self.w("scalar", cV, cV.n)
            self.q("scalar", "activation", out=st_t[:, 10:11], in_=st_t[:, 3:4],
                   func=AF.Sqrt, incs=[(cS, 1)])
            self.w("vector", cS, cS.n)
            self.q("vector", "tensor_scalar", out=st_t[:, 11:12], in0=st_t[:, 10:11],
                   scalar1=st_t[:, 10:11], scalar2=None, op0=ALU.mult)
            self.dr("vector")
            self.q("vector", "tensor_scalar", out=st_t[:, 11:12], in0=st_t[:, 11:12],
                   scalar1=st_t[:, 2:3], scalar2=None, op0=ALU.mult)
            self.dr("vector")
            self.q("vector", "tensor_scalar", out=st_t[:, 11:12], in0=st_t[:, 11:12],
                   scalar1=-0.5, scalar2=1.5, op0=ALU.mult, op1=ALU.add)
            self.dr("vector")
            self.q("vector", "tensor_scalar", out=rstd_dst, in0=st_t[:, 11:12],
                   scalar1=st_t[:, 10:11], scalar2=None, op0=ALU.mult)
            self.dr("vector")

        def round_chain(buf, r0buf, out_ap):
            self.w("scalar", cV, cV.n)
            self.q("scalar", "copy", out=r0buf, in_=buf, incs=[(cS, 1)])
            self.w("vector", cS, cS.n)
            self.q("vector", "tensor_tensor", out=buf, in0=buf, in1=r0buf,
                   op=ALU.subtract)
            self.dr("vector")
            self.q("vector", "tensor_scalar", out=buf, in0=buf,
                   scalar1=0.5, scalar2=None, op0=ALU.is_ge)
            self.dr("vector")
            self.q("vector", "scalar_tensor_tensor", out=out_ap, in0=buf,
                   scalar=-512.0, in1=r0buf, op0=ALU.add, op1=ALU.add,
                   incs=[(cV, 1)])
            self.dr("vector")
            return cV.n

        def round_wide(buf, r0buf, wbuf, cbuf, out_ap, nhalf=1, idle_cb=None):
            self.w("scalar", cV, cV.n)
            self.q("scalar", "copy", out=r0buf, in_=buf, incs=[(cS, 1)])
            self.w("vector", cS, cS.n)
            n = buf.shape[1]
            hw = n // nhalf
            for hh2 in range(nhalf):
                sl = slice(hh2 * hw, (hh2 + 1) * hw)
                self.q("vector", "scalar_tensor_tensor", out=wbuf[:, 0:hw],
                       in0=buf[:, sl], scalar=16.5, in1=r0buf[:, sl],
                       op0=ALU.add, op1=ALU.subtract, incs=[(cV, 1)])
                self.dr("vector")
                for b in (16.0, 8.0, 4.0, 2.0, 1.0):
                    self.q("vector", "tensor_scalar", out=cbuf[:, 0:hw],
                           in0=wbuf[:, 0:hw], scalar1=b, scalar2=-b,
                           op0=ALU.is_ge, op1=ALU.mult, incs=[(cV, 1)])
                    self.dr("vector")
                    self.w("gpsimd", cV, cV.n)
                    self.q("gpsimd", "tensor_tensor", out=wbuf[:, 0:hw],
                           in0=wbuf[:, 0:hw], in1=cbuf[:, 0:hw],
                           op=ALU.add, incs=[(self.cG, 1)])
                    self.dr("gpsimd")
                    if idle_cb is not None:
                        idle_cb()
                    self.w("vector", self.cG, self.cG.n)
                self.q("vector", "scalar_tensor_tensor", out=out_ap[:, sl] if nhalf > 1 else out_ap,
                       in0=buf[:, sl], scalar=-511.5, in1=wbuf[:, 0:hw],
                       op0=ALU.add, op1=ALU.subtract, incs=[(cV, 1)])
                self.dr("vector")
            return cV.n

        for dst, srca in ((k_idf, identf), (k_idb, identb),
                          (k_1cf, onescf), (k_1cb, onescb),
                          (k_gamc, gamc), (k_betc, betc)):
            self.dma("sync", dst[:], srca[:], "c0")
        self.wdma("tensor", "c0")
        self.wdma("vector", "c0")
        self.wdma("scalar", "c0")

        self.q("vector", "memset", k_1all[:], 1.0)
        self.dr("vector")
        self.q("vector", "memset", k127i[:], 127.0)
        self.dr("vector")
        self.q("vector", "reciprocal", out=k127i[:], in_=k127i[:])
        self.dr("vector")
        self.q("vector", "memset", kRT2[:], 128.0)
        self.dr("vector")
        self.q("vector", "reciprocal", out=kRT2[:], in_=kRT2[:], incs=[(cV, 1)])
        self.dr("vector")
        self.w("scalar", cV, cV.n)
        self.q("scalar", "activation", out=st_t[:, 10:11], in_=kRT2[:],
               func=AF.Sqrt, incs=[(cS, 1)])
        self.w("vector", cS, cS.n)
        self.q("vector", "tensor_scalar", out=st_t[:, 11:12], in0=st_t[:, 10:11],
               scalar1=st_t[:, 10:11], scalar2=128.0, op0=ALU.mult, op1=ALU.mult)
        self.dr("vector")
        self.q("vector", "tensor_scalar", out=st_t[:, 11:12], in0=st_t[:, 11:12],
               scalar1=-0.5, scalar2=1.5, op0=ALU.mult, op1=ALU.add)
        self.dr("vector")
        self.q("vector", "tensor_scalar", out=kRT2[:], in0=st_t[:, 11:12],
               scalar1=st_t[:, 10:11], scalar2=None, op0=ALU.mult)
        self.dr("vector")

        def weight_mean_quant(wt_ap, s_out, nchunks, width, sccol, tag,
                              lay=False):
            WS = contextlib.ExitStack()
            a = self.sb(WS, "wq_a_%s" % tag, (128, width), F32)
            b = self.sb(WS, "wq_b_%s" % tag, (128, width), F32)
            r0 = self.sb(WS, "wq_r_%s" % tag, (128, width), I16)
            q8 = self.sb(WS, "wq_q_%s" % tag, (128, width), FP8)
            nsl_w = width // 512
            acc = self.sb(WS, "wq_s_%s" % tag, (128, nchunks * nsl_w), F32)
            bufs = [a, b]
            vload = []
            for c in range(nchunks):
                bb = bufs[c % 2]
                nmL = "Lw%s%d" % (tag, c % 2)
                if c >= 2:
                    self.w("sync", cS, vload[c - 2][1])
                self.dma("sync", bb[:], wt_ap[128 * c:128 * (c + 1), :], nmL)
                self.wdma("scalar", nmL)
                for sl2 in range(nsl_w):
                    self.q("scalar", "activation", out=bb[:, 512 * sl2:512 * (sl2 + 1)],
                           in_=bb[:, 512 * sl2:512 * (sl2 + 1)], func=AF.Abs,
                           accum_out=acc[:, c * nsl_w + sl2:c * nsl_w + sl2 + 1],
                           incs=[(cS, 1)] if sl2 == nsl_w - 1 else ())
                vload.append(("s", self.cS.n))
            self.w("vector", cS, self.cS.n)
            self.q("vector", "tensor_reduce", out=st_t[:, 0:1],
                   in_=acc[:, 0:nchunks * nsl_w], op=ALU.add, axis=AX.X,
                   incs=[(cV, 1)])
            self.dr("vector")
            self.w("tensor", cV, cV.n)
            mm_bank_wait(6)
            self.q("tensor", "matmul", bank(6, 1)[0:1, :], k_1cf[:], st_t[:, 0:1],
                   start=True, stop=True, incs=[(cT, 1)])
            self.w("vector", cT, cT.n)
            self.q("vector", "tensor_scalar", out=st_t[0:1, 1:2], in0=bank(6, 1)[0:1, 0:1],
                   scalar1=1.0 / (D * FF), scalar2=EPS, op0=ALU.mult, op1=ALU.max)
            self.dr("vector")
            self.q("vector", "reciprocal", out=st_t[0:1, 2:3], in_=st_t[0:1, 1:2])
            self.dr("vector")
            self.q("vector", "nop", incs=[(cV, 1)])
            self.w("tensor", cV, cV.n)
            self.q("tensor", "matmul", bank(6, 2), k_1all[0:1, :], st_t[0:1, 1:3],
                   start=True, stop=True, incs=[(cT, 1)])
            self.w("vector", cT, cT.n)
            self.q("vector", "tensor_copy", out=st_sc[:, sccol:sccol + 2],
                   in_=bank(6, 2), incs=[(cV, 1)])
            self.dr("vector")
            self.bank_free[6] = (cV, cV.n)
            v_sc = cV.n
            vclip = []
            for c in range(nchunks):
                bb = bufs[c % 2]
                nmL = "Lw%s%d" % (tag, c % 2)
                if c >= 2:
                    self.w("sync", cV, vclip[c - 2])
                else:
                    self.w("sync", cS, vload[nchunks - 2 + c][1])
                self.dma("sync", bb[:], wt_ap[128 * c:128 * (c + 1), :], nmL)
                self.wdma("vector", nmL)
                self.w("vector", cV, v_sc)
                self.q("vector", "tensor_scalar", out=bb[:], in0=bb[:],
                       scalar1=st_sc[:, sccol + 1:sccol + 2], scalar2=512.0,
                       op0=ALU.mult, op1=ALU.add, incs=[(cV, 1)])
                self.dr("vector")
                round_chain(bb[:], r0[:], bb[:])
                self.wdma("vector", "Sw" + tag)
                self.q("vector", "tensor_scalar", out=q8[:], in0=bb[:],
                       scalar1=1.0, scalar2=-1.0, op0=ALU.min, op1=ALU.max,
                       incs=[(cV, 1)])
                self.dr("vector")
                vclip.append(cV.n)
                self.w("scalar", cV, cV.n)
                if lay:
                    self.dma("scalar", s_out[:, c * width:(c + 1) * width],
                             q8[:], "Sw" + tag)
                else:
                    self.dma("scalar", s_out[128 * c:128 * (c + 1), :], q8[:],
                             "Sw" + tag)
            WS.close()
            self.barrier()
            return v_sc, None

        def mk_store(nm):
            def store(c, hi, lo, sem):
                for h in range(H):
                    self.dma("scalar", s_wh[nm][h, 0, c],
                             hi[:, 128 * h:128 * (h + 1)], sem)
                    self.dma("scalar", s_wh[nm][h, 1, c],
                             lo[:, 128 * h:128 * (h + 1)], sem)
            return store

        def store_o(c, hi, lo, sem):
            for ds in range(4):
                self.dma("scalar", s_woh[ds, 0, c],
                         hi[:, 512 * ds:512 * (ds + 1)], sem)
                self.dma("scalar", s_woh[ds, 1, c],
                         lo[:, 512 * ds:512 * (ds + 1)], sem)

        split_specs = [(wt["q"], "q", mk_store("q")), (wt["k"], "k", mk_store("k")),
                       (wt["v"], "v", mk_store("v")), (wt["o"], "o", store_o)]

        def make_split_emitters(a, b, hi_, lo_):
            st = {"v": [], "spm": []}
            bufs = [a, b]
            flat = [(wt_ap, tag, store_fn, c)
                    for wt_ap, tag, store_fn in split_specs
                    for c in range(ND)]
            n = len(flat)

            def load(cc):
                wt_ap, _, _, c = flat[cc]
                if cc >= 2:
                    self.w("gpsimd", cV, st["v"][cc - 2])
                self.dma("gpsimd", bufs[cc % 2][:],
                         wt_ap[128 * c:128 * (c + 1), :], "Lsp%d" % (cc % 2))

            def em(cc):
                if cc == 0:
                    load(0)
                if cc + 1 < n:
                    load(cc + 1)
                _, tag, store_fn, c = flat[cc]
                bb = bufs[cc % 2]
                hi = hi_[cc % 2]
                lo = lo_[cc % 2]
                self.wdma("vector", "Lsp%d" % (cc % 2))
                self.wdma("vector", "Sspill%d" % (cc % 2))
                self.q("vector", "tensor_copy", out=hi[:], in_=bb[:])
                self.dr("vector")
                self.q("vector", "scalar_tensor_tensor", out=lo[:],
                       in0=hi[:], scalar=-1.0, in1=bb[:],
                       op0=ALU.mult, op1=ALU.add, incs=[(cV, 1)])
                self.dr("vector")
                st["v"].append(cV.n)
                self.w("scalar", cV, cV.n)
                store_fn(c, hi, lo, "Sspill%d" % (cc % 2))
            return [(lambda cc=cc: em(cc)) for cc in range(n)]

        def make_w2q_emitters(a, b, r0, q8, acc):
            nsl_w = D // 512
            st = {"vload": [], "vclip": [], "vsc": 0}
            bufs = [a, b]
            ems = []

            def load1(c):
                if c >= 2:
                    self.w("sync", cS, st["vload"][c - 2])
                self.dma("sync", bufs[c % 2][:],
                         w2t[128 * c:128 * (c + 1), :], "Lww2%d" % (c % 2))

            for c in range(NFC):
                def em(c=c):
                    if c == 0:
                        load1(0)
                    if c + 1 < NFC:
                        load1(c + 1)
                    bb = bufs[c % 2]
                    self.wdma("scalar", "Lww2%d" % (c % 2))
                    for sl2 in range(nsl_w):
                        self.q("scalar", "activation",
                               out=bb[:, 512 * sl2:512 * (sl2 + 1)],
                               in_=bb[:, 512 * sl2:512 * (sl2 + 1)], func=AF.Abs,
                               accum_out=acc[:, c * nsl_w + sl2:c * nsl_w + sl2 + 1],
                               incs=[(cS, 1)] if sl2 == nsl_w - 1 else ())
                    st["vload"].append(cS.n)
                ems.append(em)

            def fin():
                self.w("vector", cS, st["vload"][-1])
                self.q("vector", "tensor_reduce", out=st_bn[:, 0:1],
                       in_=acc[:, 0:NFC * nsl_w], op=ALU.add, axis=AX.X,
                       incs=[(cV, 1)])
                self.dr("vector")
                self.w("tensor", cV, cV.n)
                mm_bank_wait(6)
                self.q("tensor", "matmul", bank(6, 1)[0:1, :], k_1cf[:],
                       st_bn[:, 0:1], start=True, stop=True, incs=[(cT, 1)])
                self.w("vector", cT, cT.n)
                self.q("vector", "tensor_scalar", out=st_bn[0:1, 1:2],
                       in0=bank(6, 1)[0:1, 0:1], scalar1=1.0 / (D * FF),
                       scalar2=EPS, op0=ALU.mult, op1=ALU.max)
                self.dr("vector")
                self.q("vector", "reciprocal", out=st_bn[0:1, 2:3],
                       in_=st_bn[0:1, 1:2])
                self.dr("vector")
                self.q("vector", "nop", incs=[(cV, 1)])
                self.w("tensor", cV, cV.n)
                self.q("tensor", "matmul", bank(6, 2), k_1all[0:1, :],
                       st_bn[0:1, 1:3], start=True, stop=True, incs=[(cT, 1)])
                self.w("vector", cT, cT.n)
                self.q("vector", "tensor_copy", out=st_sc[:, 2:4],
                       in_=bank(6, 2), incs=[(cV, 1)])
                self.dr("vector")
                self.bank_free[6] = (cV, cV.n)
                st["vsc"] = cV.n
            ems.append(fin)

            def load2(c):
                if c >= 2:
                    self.w("sync", cV, st["vclip"][c - 2])
                else:
                    self.w("sync", cS, st["vload"][NFC - 2 + c])
                self.dma("sync", bufs[c % 2][:],
                         w2t[128 * c:128 * (c + 1), :], "Lww2%d" % (c % 2))

            for c in range(NFC):
                def em(c=c):
                    if c == 0:
                        load2(0)
                    if c + 1 < NFC:
                        load2(c + 1)
                    bb = bufs[c % 2]
                    self.wdma("vector", "Lww2%d" % (c % 2))
                    self.w("vector", cV, st["vsc"])
                    self.q("vector", "tensor_scalar", out=bb[:], in0=bb[:],
                           scalar1=st_sc[:, 3:4], scalar2=512.0,
                           op0=ALU.mult, op1=ALU.add, incs=[(cV, 1)])
                    self.dr("vector")
                    round_chain(bb[:], r0[:], bb[:])
                    self.wdma("vector", "Sww2")
                    self.q("vector", "tensor_scalar", out=q8[:], in0=bb[:],
                           scalar1=1.0, scalar2=-1.0, op0=ALU.min, op1=ALU.max,
                           incs=[(cV, 1)])
                    self.dr("vector")
                    st["vclip"].append(cV.n)
                    self.w("scalar", cV, cV.n)
                    self.dma("scalar", s_w2q[:, c * D:(c + 1) * D], q8[:],
                             "Sww2")
                ems.append(em)
            return ems

        v_sc1, _ = weight_mean_quant(w1t, s_w1q, ND, FF, 0, "w1")
        if "ffn1" not in self.parts:
            if "ffn2" in self.parts:
                weight_mean_quant(w2t, s_w2q, NFC, D, 2, "w2", lay=True)
            if "attn" in self.parts:
                WS = contextlib.ExitStack()
                a = self.sb(WS, "ws_a", (128, D), F32)
                b = self.sb(WS, "ws_b", (128, D), F32)
                hi = [self.sb(WS, "ws_h%d" % i, (128, D), BF16)
                      for i in range(2)]
                lo = [self.sb(WS, "ws_l%d" % i, (128, D), BF16)
                      for i in range(2)]
                for em in make_split_emitters(a, b, hi, lo):
                    em()
                WS.close()
                self.barrier()

        def ffn(x_src, x_next_dst, cid):
            tg = "f%d" % cid
            prep = (cid == 0 and "ffn1" in self.parts)
            CALL = contextlib.ExitStack()
            k_b2B = self.sb(CALL, "k_b2B%s" % tg, (128, 2048), F32)
            rstg = [self.sb(CALL, "rstg%d%s" % (i, tg), (1, 512), F32)
                    for i in range(2)]
            st_mu = self.sb(CALL, "st_mu%s" % tg, (128, NT), F32)
            st_rstd = self.sb(CALL, "st_rstd%s" % tg, (128, NT), F32)
            st_m2 = self.sb(CALL, "st_m2%s" % tg, (128, NT), F32)
            st_sv = self.sb(CALL, "st_sv%s" % tg, (128, NT), F32)
            st_sv2 = self.sb(CALL, "st_sv2%s" % tg, (128, NT), F32)

            last_bc = {}

            def bcast_row(dram_row, bk_i, stg_i, wdt=512):
                nmr = "Lr%s%d" % (tg, stg_i)
                if stg_i in last_bc:
                    self.w("sync", cT, last_bc[stg_i])
                self.dma("sync", rstg[stg_i][0:1, :wdt], dram_row, nmr)
                self.wdma("tensor", nmr)
                mm_bank_wait(bk_i)
                v = pe_bcast(bank(bk_i, wdt), rstg[stg_i][0:1, :wdt])
                last_bc[stg_i] = v
                return v

            for j in range(4):
                bcast_row(b2r[j:j + 1, :], 2 + j, j % 2)
            tvb = cT.n
            self.w("vector", cT, tvb)
            for j in range(4):
                self.q("vector", "tensor_scalar", out=k_b2B[:, 512 * j:512 * (j + 1)],
                       in0=bank(2 + j), scalar1=0.5, scalar2=None, op0=ALU.mult,
                       incs=[(cV, 1)] if j == 3 else ())
                self.dr("vector")
            for j in range(4):
                self.bank_free[2 + j] = (cV, cV.n)

            HB = contextlib.ExitStack()
            hbn = self.sb(HB, "hbn%s" % tg, (128, 6 * NFS * NT), F32)

            SX = contextlib.ExitStack()
            xqT = self.sb(SX, "xqT%s" % tg, (128, ND * T), BF16)
            xqT3 = xqT[:].rearrange("p (c t2) -> p c t2", c=ND)
            SA = contextlib.ExitStack()
            xb_ = [self.sb(SA, "fa_x%d%s" % (i, tg), (128, D), F32) for i in range(2)]
            db = self.sb(SA, "fa_d%s" % tg, (128, D), F32)
            ub = self.sb(SA, "fa_u%s" % tg, (128, D), F32)
            r0b = self.sb(SA, "fa_r%s" % tg, (128, D), I16)
            iqb = self.sb(SA, "fa_q%s" % tg, (128, D), BF16)
            SP = contextlib.ExitStack()
            il_mm1 = ()
            if prep:
                sp_a = self.sb(SP, "sp_a", (128, D), F32)
                sp_b = self.sb(SP, "sp_b", (128, D), F32)
                sp_h = [self.sb(SP, "sp_h%d" % i, (128, D), BF16)
                        for i in range(2)]
                sp_l = [self.sb(SP, "sp_l%d" % i, (128, D), BF16)
                        for i in range(2)]
                il_mm1 = make_split_emitters(sp_a, sp_b, sp_h, sp_l)
            SM = contextlib.ExitStack()
            NSLAB = 4
            wsl = [self.sb(SM, "w1s%d%s" % (i, tg), (128, ND * 512), FP8)
                   for i in range(NSLAB)]
            gb_ = [self.sb(SM, "g%d%s" % (i, tg), (128, 512), F32) for i in range(3)]
            b1B = [self.sb(SM, "b1B%d%s" % (i, tg), (128, 512), F32) for i in range(2)]
            s_w1q3 = s_w1q.rearrange("(c p) f -> p c f", p=128)
            v_slab_ok = {}

            slab_engs = (("sync", 0, 6), ("scalar", 6, 11), ("gpsimd", 11, 16))

            def slab_load(fs):
                wv = wsl[fs % NSLAB][:].rearrange("p (c w2) -> p c w2", w2=512)
                for eng, c0, c1 in slab_engs:
                    if fs >= NSLAB:
                        self.w(eng, cT, v_slab_ok[fs - NSLAB])
                    self.dma(eng, wv[:, c0:c1, :],
                             s_w1q3[:, c0:c1, 512 * fs:512 * (fs + 1)],
                             "Ls%s%d%s" % (tg, fs % NSLAB, eng[0]))

            def prep_slab(fs):
                wb = wsl[fs % NSLAB]
                nms = "Ls%s%d" % (tg, fs % NSLAB)
                bcast_row(b1r[fs:fs + 1, :], 6, fs % 2)
                if fs + NSLAB - 1 < NFS:
                    slab_load(fs + NSLAB - 1)
                self.w("vector", cT, cT.n)
                self.q("vector", "tensor_copy", out=b1B[fs % 2][:], in_=bank(6),
                       incs=[(cV, 1)])
                self.dr("vector")
                self.bank_free[6] = (cV, cV.n)
                return wb, nms, cV.n

            for _fs0 in range(NSLAB - 1):
                slab_load(_fs0)

            def mm1_tile(fs, t, wb, nms, v_b1, v_xq):
                bk = (fs * NT + t) % 2
                gsl = (fs * NT + t) % 3
                gb = gb_[gsl]
                if t == 0:
                    for eng, _, _ in slab_engs:
                        self.wdma("tensor", nms + eng[0])
                self.w("tensor", cV, v_xq[t])
                mm_bank_wait(bk)
                for c in range(ND):
                    self.q("tensor", "matmul", bank(bk),
                           xqT[:, c * T + 128 * t:c * T + 128 * (t + 1)],
                           wb[:, 512 * c:512 * (c + 1)],
                           start=(c == 0), stop=(c == ND - 1),
                           incs=[(cT, 1)] if c == ND - 1 else ())
                if t == NT - 1:
                    v_slab_ok[fs] = cT.n
                self.w("vector", cT, cT.n)
                self.w("vector", cV, v_b1)
                self.wdma("vector", "Sg%s%d" % (tg, gsl))
                self.q("vector", "scalar_tensor_tensor", out=gb[:], in0=bank(bk),
                       scalar=st_sv[:, t:t + 1], in1=b1B[fs % 2][:],
                       op0=ALU.mult, op1=ALU.add, incs=[(cV, 1)])
                self.dr("vector")
                self.bank_free[bk] = (cV, cV.n)
                self.w("scalar", cV, cV.n)
                self.q("scalar", "activation", out=gb[:], in_=gb[:], func=AF.Gelu,
                       incs=[(cS, 1)])
                self.w("vector", cS, cS.n)
                self.q("vector", "bn_stats",
                       out=hbn[:, 6 * (t * NFS + fs):6 * (t * NFS + fs) + 6],
                       in_=gb[:], incs=[(cV, 1)])
                self.dr("vector")
                self.w("scalar", cV, cV.n)
                self.dma("scalar",
                         s_g[128 * t:128 * (t + 1), 512 * fs:512 * (fs + 1)],
                         gb[:], "Sg%s%d" % (tg, gsl))

            slab0 = prep_slab(0)
            v_xq = []
            v_dstep = []
            for t in range(NT):
                xb = xb_[t % 2]
                nmx = "Lx%s%d" % (tg, t % 2)
                if t >= 2:
                    self.w("sync", cV, v_dstep[t - 2])
                self.dma("sync", xb[:], x_src[128 * t:128 * (t + 1), :], nmx)
                self.wdma("vector", nmx)
                ln_stats_tile(xb, st_mu[:, t:t + 1], st_rstd[:, t:t + 1])
                self.q("vector", "tensor_scalar", out=db[:], in0=xb[:],
                       scalar1=st_mu[:, t:t + 1], scalar2=0.0,
                       op0=ALU.subtract, op1=ALU.max, accum_out=st_t[:, 4:5],
                       incs=[(cV, 1)])
                self.dr("vector")
                v_dstep.append(cV.n)
                self.q("vector", "tensor_scalar", out=ub[:], in0=db[:],
                       scalar1=-1.0, scalar2=0.0,
                       op0=ALU.mult, op1=ALU.max, accum_out=st_t[:, 9:10])
                self.dr("vector")
                self.q("vector", "tensor_tensor", out=st_t[:, 4:5], in0=st_t[:, 4:5],
                       in1=st_t[:, 9:10], op=ALU.max)
                self.dr("vector")
                self.q("vector", "tensor_scalar", out=st_m2[:, t:t + 1], in0=st_t[:, 4:5],
                       scalar1=st_rstd[:, t:t + 1], scalar2=EPS,
                       op0=ALU.mult, op1=ALU.max)
                self.dr("vector")
                self.q("vector", "reciprocal", out=st_t[:, 5:6], in_=st_m2[:, t:t + 1])
                self.dr("vector")
                self.q("vector", "tensor_scalar", out=st_t[:, 6:7], in0=st_t[:, 5:6],
                       scalar1=st_rstd[:, t:t + 1], scalar2=127.0,
                       op0=ALU.mult, op1=ALU.mult)
                self.dr("vector")
                self.q("vector", "tensor_scalar", out=ub[:], in0=db[:],
                       scalar1=st_t[:, 6:7], scalar2=512.0, op0=ALU.mult, op1=ALU.add,
                       incs=[(cV, 1)])
                self.dr("vector")
                if cid == 0:
                    round_chain(ub[:], r0b[:], iqb[:])
                else:
                    round_wide(ub[:], r0b[:], db[:], xb[:], iqb[:])
                v_iq = self.cV.n
                v_dstep[-1] = v_iq
                self.w("vector", cV, v_sc1)
                self.q("vector", "tensor_scalar", out=st_sv[:, t:t + 1],
                       in0=st_m2[:, t:t + 1], scalar1=st_sc[:, 0:1],
                       scalar2=k127i[:, 0:1], op0=ALU.mult, op1=ALU.mult)
                self.dr("vector")
                for g2 in range(2):
                    self.w("tensor", cV, v_iq)
                    if ("psH", 0) in self.bank_free:
                        c_, v_ = self.bank_free[("psH", 0)]
                        self.w("tensor", c_, v_)
                    for j in range(8):
                        c = g2 * 8 + j
                        self.q("tensor", "transpose", psH[:, 128 * j:128 * (j + 1)],
                               iqb[:, 128 * c:128 * (c + 1)], k_idb[:],
                               incs=[(cT, 1)] if j == 7 else ())
                    self.w("vector", cT, cT.n)
                    self.q("vector", "tensor_copy",
                           out=xqT3[:, g2 * 8:(g2 + 1) * 8, 128 * t:128 * (t + 1)],
                           in_=psH[:].rearrange("p (j t2) -> p j t2", j=8),
                           incs=[(cV, 1)])
                    self.dr("vector")
                    self.bank_free[("psH", 0)] = (cV, cV.n)
                v_xq.append(cV.n)
                mm1_tile(0, t, slab0[0], slab0[1], slab0[2], v_xq)

            mm1_queue = list(il_mm1)
            for fs in range(1, NFS):
                wb, nms, v_b1 = prep_slab(fs)
                for t in range(NT):
                    mm1_tile(fs, t, wb, nms, v_b1, v_xq)
                    if t % 3 == 2 and mm1_queue:
                        mm1_queue.pop(0)()
            while mm1_queue:
                mm1_queue.pop(0)()
            SM.close()
            SP.close()
            SA.close()
            SX.close()
            self.barrier()

            W2Q = contextlib.ExitStack()
            il_pb = ()
            if prep:
                w2_a = self.sb(W2Q, "w2_a", (128, D), F32)
                w2_b = self.sb(W2Q, "w2_b", (128, D), F32)
                w2_r = self.sb(W2Q, "w2_r", (128, D), I16)
                w2_q = self.sb(W2Q, "w2_q", (128, D), FP8)
                w2_s = self.sb(W2Q, "w2_s", (128, NFC * (D // 512)), F32)
                il_pb = make_w2q_emitters(w2_a, w2_b, w2_r, w2_q, w2_s)
            SB = contextlib.ExitStack()
            gt = self.sb(SB, "pb_g%s" % tg, (128, FF), F32)
            gamBF = self.sb(SB, "pb_gB%s" % tg, (128, FF), F32)
            betBF = self.sb(SB, "pb_bB%s" % tg, (128, FF), F32)
            r0h = self.sb(SB, "pb_r%s" % tg, (128, FF), I16)
            iqh = self.sb(SB, "pb_q%s" % tg, (128, FF), BF16)
            pbw = self.sb(SB, "pb_w%s" % tg, (128, FF // 4), F32)
            pbc = self.sb(SB, "pb_c%s" % tg, (128, FF // 4), F32)
            hqc = self.sb(SB, "pb_h%s" % tg, (128, NFC * 128), BF16)
            hqc3 = hqc[:].rearrange("p (c t2) -> p c t2", c=NFC)
            for fs in range(NFS):
                bcast_row(gFr[fs:fs + 1, :], 0, 0)
                self.w("vector", cT, cT.n)
                self.q("vector", "tensor_copy", out=gamBF[:, 512 * fs:512 * (fs + 1)],
                       in_=bank(0), incs=[(cV, 1)])
                self.dr("vector")
                self.bank_free[0] = (cV, cV.n)
                bcast_row(bFr[fs:fs + 1, :], 1, 1)
                self.w("vector", cT, cT.n)
                self.q("vector", "tensor_copy", out=betBF[:, 512 * fs:512 * (fs + 1)],
                       in_=bank(1), incs=[(cV, 1)])
                self.dr("vector")
                self.bank_free[1] = (cV, cV.n)
            pb_queue = list(il_pb)

            def pb_idle():
                if pb_queue:
                    pb_queue.pop(0)()
            v_iqh_prev = 0
            for t in range(NT):
                if t >= 1:
                    self.w("sync", cV, v_iqh_prev)
                self.dma("sync", gt[:], s_g[128 * t:128 * (t + 1), :], "Lg" + tg)
                self.q("vector", "bn_aggr", out=st_t[:, 0:2],
                       in_=hbn[:, 6 * (t * NFS):6 * (t * NFS) + 6 * NFS])
                self.dr("vector")
                self.q("vector", "tensor_scalar", out=st_t[:, 2:3], in0=st_t[:, 1:2],
                       scalar1=EPS, scalar2=None, op0=ALU.add)
                self.dr("vector")
                self.q("vector", "reciprocal", out=st_t[:, 3:4], in_=st_t[:, 2:3],
                       incs=[(cV, 1)])
                self.dr("vector")
                self.w("scalar", cV, cV.n)
                self.q("scalar", "activation", out=st_t[:, 10:11], in_=st_t[:, 3:4],
                       func=AF.Sqrt, incs=[(cS, 1)])
                self.w("vector", cS, cS.n)
                self.q("vector", "tensor_scalar", out=st_t[:, 11:12], in0=st_t[:, 10:11],
                       scalar1=st_t[:, 10:11], scalar2=None, op0=ALU.mult)
                self.dr("vector")
                self.q("vector", "tensor_scalar", out=st_t[:, 11:12], in0=st_t[:, 11:12],
                       scalar1=st_t[:, 2:3], scalar2=None, op0=ALU.mult)
                self.dr("vector")
                self.q("vector", "tensor_scalar", out=st_t[:, 11:12], in0=st_t[:, 11:12],
                       scalar1=-0.5, scalar2=1.5, op0=ALU.mult, op1=ALU.add)
                self.dr("vector")
                self.q("vector", "tensor_scalar", out=st_t[:, 4:5], in0=st_t[:, 11:12],
                       scalar1=st_t[:, 10:11], scalar2=None, op0=ALU.mult)
                self.dr("vector")
                self.wdma("vector", "Lg" + tg)
                self.q("vector", "tensor_scalar", out=gt[:], in0=gt[:],
                       scalar1=st_t[:, 0:1], scalar2=st_t[:, 4:5],
                       op0=ALU.subtract, op1=ALU.mult)
                self.dr("vector")
                self.q("vector", "scalar_tensor_tensor", out=gt[:], in0=gt[:],
                       scalar=1.0, in1=gamBF[:], op0=ALU.mult, op1=ALU.mult)
                self.dr("vector")
                self.q("vector", "tensor_tensor", out=gt[:], in0=gt[:], in1=betBF[:],
                       op=ALU.add)
                self.dr("vector")
                self.q("vector", "tensor_scalar", out=iqh[:], in0=gt[:],
                       scalar1=1.0, scalar2=0.0, op0=ALU.mult, op1=ALU.max,
                       accum_out=st_t[:, 5:6])
                self.dr("vector")
                self.q("vector", "tensor_scalar", out=iqh[:], in0=gt[:],
                       scalar1=-1.0, scalar2=0.0, op0=ALU.mult, op1=ALU.max,
                       accum_out=st_t[:, 9:10])
                self.dr("vector")
                self.q("vector", "tensor_tensor", out=st_t[:, 5:6], in0=st_t[:, 5:6],
                       in1=st_t[:, 9:10], op=ALU.max)
                self.dr("vector")
                self.q("vector", "tensor_scalar", out=st_t[:, 6:7], in0=st_t[:, 5:6],
                       scalar1=EPS, scalar2=None, op0=ALU.max)
                self.dr("vector")
                self.q("vector", "reciprocal", out=st_t[:, 7:8], in_=st_t[:, 6:7])
                self.dr("vector")
                self.q("vector", "tensor_scalar", out=st_t[:, 8:9], in0=st_t[:, 7:8],
                       scalar1=127.0, scalar2=None, op0=ALU.mult)
                self.dr("vector")
                self.q("vector", "tensor_copy", out=st_sv2[:, t:t + 1],
                       in_=st_t[:, 6:7])
                self.dr("vector")
                self.q("vector", "tensor_scalar", out=gt[:], in0=gt[:],
                       scalar1=st_t[:, 8:9], scalar2=512.0, op0=ALU.mult, op1=ALU.add,
                       incs=[(cV, 1)])
                self.dr("vector")
                v_iqh = round_wide(gt[:], r0h[:], pbw[:], pbc[:], iqh[:],
                                   nhalf=4, idle_cb=pb_idle)
                v_iqh_prev = v_iqh
                for g8 in range(8):
                    self.w("tensor", cV, v_iqh)
                    if ("psH", 0) in self.bank_free:
                        c_, v_ = self.bank_free[("psH", 0)]
                        self.w("tensor", c_, v_)
                    for j in range(8):
                        c = g8 * 8 + j
                        self.q("tensor", "transpose", psH[:, 128 * j:128 * (j + 1)],
                               iqh[:, 128 * c:128 * (c + 1)], k_idb[:],
                               incs=[(cT, 1)] if j == 7 else ())
                    self.w("scalar", cT, cT.n)
                    if g8 == 0:
                        self.wdma("scalar", "Shq" + tg)
                    self.q("scalar", "activation",
                           out=hqc3[:, g8 * 8:(g8 + 1) * 8, :],
                           in_=psH[:].rearrange("p (j t2) -> p j t2", j=8),
                           func=AF.Copy, incs=[(cS, 1)])
                    self.bank_free[("psH", 0)] = (cS, cS.n)
                self.dr("scalar")
                self.dma("scalar", s_hq[t], hqc[:], "Shq" + tg)
            while pb_queue:
                pb_queue.pop(0)()
            self.q("vector", "tensor_scalar", out=st_sv2[:, 0:NT],
                   in0=st_sv2[:, 0:NT], scalar1=st_sc[:, 2:3],
                   scalar2=k127i[:, 0:1], op0=ALU.mult, op1=ALU.mult)
            self.dr("vector")
            self.q("vector", "tensor_scalar", out=st_sv2[:, 0:NT],
                   in0=st_sv2[:, 0:NT], scalar1=0.5, scalar2=None,
                   op0=ALU.mult)
            self.dr("vector")
            SB.close()
            W2Q.close()
            HB.close()
            self.barrier()

            SC = contextlib.ExitStack()
            w2res = self.sb(SC, "w2r%s" % tg, (128, NFC * D), FP8)
            hq_ = [self.sb(SC, "hq%d%s" % (i, tg), (128, NFC * 128), BF16)
                   for i in range(2)]
            xr = self.sb(SC, "xr%s" % tg, (128, D), F32)
            xo = self.sb(SC, "xo%s" % tg, (128, D), F32)
            self.wdma("sync", "Sww2")
            self.dma("sync", w2res[:], s_w2q[:], "Lw2" + tg)
            v_ep = {}
            for t in range(NT):
                hsl = t % 2
                hq = hq_[hsl]
                nmh = "Lh%s%d" % (tg, hsl)
                if t >= 2:
                    self.w("sync", cV, v_ep[t - 2])
                if t >= 1:
                    self.w("sync", cV, v_ep[t - 1])
                self.dma("sync", hq[:], s_hq[t], nmh)
                self.dma("sync", xr[:], x_src[128 * t:128 * (t + 1), :], "Lxr" + tg)
                self.wdma("vector", "Lxr" + tg)
                self.wdma("vector", "Sxn" + tg)
                self.q("vector", "tensor_tensor", out=xo[:], in0=xr[:], in1=k_b2B[:],
                       op=ALU.add, incs=[(cV, 1)])
                self.dr("vector")
                v_xo = cV.n
                for ds in range(4):
                    bk = ds % 2 + 2
                    mm_bank_wait(bk)
                    self.wdma("tensor", "Lw2" + tg)
                    self.wdma("tensor", nmh)
                    for c in range(NFC):
                        self.q("tensor", "matmul", bank(bk),
                               hq[:, 128 * c:128 * (c + 1)],
                               w2res[:, c * D + 512 * ds:c * D + 512 * (ds + 1)],
                               start=(c == 0), stop=(c == NFC - 1),
                               incs=[(cT, 1)] if c == NFC - 1 else ())
                    self.w("vector", cT, cT.n)
                    self.w("vector", cV, v_xo)
                    self.q("vector", "scalar_tensor_tensor",
                           out=xo[:, 512 * ds:512 * (ds + 1)], in0=bank(bk),
                           scalar=st_sv2[:, t:t + 1],
                           in1=xo[:, 512 * ds:512 * (ds + 1)],
                           op0=ALU.mult, op1=ALU.add, incs=[(cV, 1)])
                    self.dr("vector")
                    self.bank_free[bk] = (cV, cV.n)
                v_ep[t] = cV.n
                self.w("scalar", cV, cV.n)
                self.dma("scalar", x_next_dst[128 * t:128 * (t + 1), :],
                         xo[:], "Sxn" + tg)
            SC.close()
            CALL.close()
            self.barrier()

        def attn(x_src, x_next_dst):
            ACALL = contextlib.ExitStack()
            k_m4 = self.sb(ACALL, "k_m4", (128, 896), BF16)
            self.dma("sync", k_m4[:], mask4[:], "Lm4")
            st_amu = self.sb(ACALL, "st_amu", (128, NT), F32)
            st_ars = self.sb(ACALL, "st_ars", (128, NT), F32)

            SX = contextlib.ExitStack()
            xnT_hi = self.sb(SX, "xnT_hi", (128, ND * T), BF16)
            xnT_lo = self.sb(SX, "xnT_lo", (128, ND * T), BF16)
            xnT_hi3 = xnT_hi[:].rearrange("p (c t2) -> p c t2", c=ND)
            xnT_lo3 = xnT_lo[:].rearrange("p (c t2) -> p c t2", c=ND)
            SA = contextlib.ExitStack()
            xb_ = [self.sb(SA, "aa_x%d" % i, (128, D), F32) for i in range(2)]
            v_xn = []
            for t in range(NT):
                xb = xb_[t % 2]
                nmx = "Lax%d" % (t % 2)
                if t >= 2:
                    self.w("sync", cV, v_xn[t - 2])
                self.dma("sync", xb[:], x_src[128 * t:128 * (t + 1), :], nmx)
                self.wdma("vector", nmx)
                ln_stats_tile(xb, st_amu[:, t:t + 1], st_ars[:, t:t + 1])
                self.q("vector", "tensor_scalar", out=xb[:], in0=xb[:],
                       scalar1=st_amu[:, t:t + 1], scalar2=st_ars[:, t:t + 1],
                       op0=ALU.subtract, op1=ALU.mult, incs=[(cV, 1)])
                self.dr("vector")
                v_xnb = cV.n
                for g4 in range(4):
                    self.w("tensor", cV, v_xnb)
                    mm_bank_wait(0)
                    for j in range(4):
                        c = g4 * 4 + j
                        self.q("tensor", "transpose", psF[:, 128 * j:128 * (j + 1)],
                               xb[:, 128 * c:128 * (c + 1)], k_idf[:],
                               incs=[(cT, 1)] if j == 3 else ())
                    self.w("vector", cT, cT.n)
                    ps4 = bank(0).rearrange("p (j t2) -> p j t2", j=4)
                    self.q("vector", "tensor_copy",
                           out=xnT_hi3[:, g4 * 4:(g4 + 1) * 4, 128 * t:128 * (t + 1)],
                           in_=ps4)
                    self.dr("vector")
                    self.q("vector", "scalar_tensor_tensor",
                           out=xnT_lo3[:, g4 * 4:(g4 + 1) * 4, 128 * t:128 * (t + 1)],
                           in0=xnT_hi3[:, g4 * 4:(g4 + 1) * 4, 128 * t:128 * (t + 1)],
                           scalar=-1.0, in1=ps4, op0=ALU.mult, op1=ALU.add,
                           incs=[(cV, 1)])
                    self.dr("vector")
                    self.bank_free[0] = (cV, cV.n)
                v_xn.append(cV.n)
            SA.close()

            SH = contextlib.ExitStack()
            qT = {hl: self.sb(SH, "a_q%d" % hl, (128, T), BF16) for hl in range(2)}
            kT = {hl: self.sb(SH, "a_k%d" % hl, (128, T), BF16) for hl in range(2)}
            vB = {hl: self.sb(SH, "a_v%d" % hl, (128, T), BF16) for hl in range(2)}
            wsl_ = [(self.sb(SH, "a_wh%d" % i, (128, ND * 128), BF16),
                     self.sb(SH, "a_wl%d" % i, (128, ND * 128), BF16))
                    for i in range(2)]
            yb_ = {nm: self.sb(SH, "a_y" + nm, (128, T), F32)
                   for nm in ("q", "k")}
            yb_["v"] = yb_["q"]
            NSL = T // 512
            syr = self.sb(SH, "a_sy", (128, 512), F32)
            sy2r = self.sb(SH, "a_sy2", (128, 512), F32)
            mur = self.sb(SH, "a_mu", (128, 512), F32)
            rsr = self.sb(SH, "a_rs", (128, 512), F32)
            for _buf in (syr, sy2r, mur, rsr):
                self.q("vector", "memset", _buf[:], 0.0)
                self.dr("vector")
            s2f_ = [self.sb(SH, "a_s2f%d" % i, (128, 512), F32) for i in range(2)]
            s2h_ = [self.sb(SH, "a_s2h%d" % i, (128, 512), BF16) for i in range(2)]
            mu0 = self.sb(SH, "a_mu0", (1, 512), F32)
            rs0 = self.sb(SH, "a_rs0", (1, 512), F32)
            s2l_ = [self.sb(SH, "a_s2l%d" % i, (128, 512), BF16) for i in range(2)]
            zrow = rsr
            if T >= 1024:
                rzB = yb_["k"][:, 0:512]
                ob = yb_["k"][:, 512:1024]
            else:
                rzB = self.sb(SH, "a_rz", (128, 512), F32)[:]
                ob = self.sb(SH, "a_ob", (128, 512), F32)[:]
            obh = s2h_[0]
            obl = s2l_[0]

            slab_used = [0, 0]
            self.wdma("sync", "Sspill0")
            self.wdma("sync", "Sspill1")

            def load_slabs(si, nm, h):
                if h == 0:
                    self.w("sync", cV, v_xn[min(3, NT - 1)])
                self.w("sync", cT, slab_used[si])
                wslh, wsll = wsl_[si]
                for c in range(ND):
                    self.dma("sync", wslh[:, 128 * c:128 * (c + 1)],
                             s_wh[nm][h, 0, c], "Lws%d" % si)
                    self.dma("sync", wsll[:, 128 * c:128 * (c + 1)],
                             s_wh[nm][h, 1, c], "Lws%d" % si)

            def proj_mm(si, nm):
                wslh, wsll = wsl_[si]
                yb = yb_[nm]
                v_y = []
                s_sq = []

                def stat_pair(ts):
                    if ts == 0:
                        mm_bank_wait(5)
                        mm_bank_wait(6)
                    if ts == 3:
                        mm_bank_wait(4)
                        mm_bank_wait(2)
                    sl = slice(512 * ts, 512 * (ts + 1))
                    self.w("tensor", cV, v_y[ts])
                    self.w("tensor", cS, s_sq[ts])
                    oy = (psF[32 * ts:32 * ts + 1, 3072:3584] if ts < 3
                          else psF[0:1, 2048:2560])
                    oy2 = (psF[32 * ts:32 * ts + 1, 2560:3072] if ts < 3
                           else psF[0:1, 1024:1536])
                    self.q("tensor", "matmul", oy, k_1cf[:], yb[:, sl],
                           start=True, stop=True)
                    self.q("tensor", "matmul", oy2,
                           k_1cf[:], s2f_[ts % 2][:], start=True, stop=True,
                           incs=[(cT, 1)] if ts == NSL - 1 else ())

                for ts in range(NSL):
                    if self.proj_off:
                        break
                    bk = ts % 2
                    mm_bank_wait(bk)
                    self.wdma("tensor", "Lws%d" % si)
                    self.w("tensor", cV, v_xn[min(4 * ts + 3, NT - 1)])
                    for c in range(ND):
                        a0 = c * T + 512 * ts
                        self.q("tensor", "matmul", bank(bk),
                               wslh[:, 128 * c:128 * (c + 1)],
                               xnT_hi[:, a0:a0 + 512], start=(c == 0), stop=False)
                        self.q("tensor", "matmul", bank(bk),
                               wsll[:, 128 * c:128 * (c + 1)],
                               xnT_hi[:, a0:a0 + 512], start=False, stop=False)
                        self.q("tensor", "matmul", bank(bk),
                               wslh[:, 128 * c:128 * (c + 1)],
                               xnT_lo[:, a0:a0 + 512], start=False,
                               stop=(c == ND - 1),
                               incs=[(cT, 1)] if c == ND - 1 else ())
                    if ts == NSL - 1:
                        slab_used[si] = cT.n
                    self.w("vector", cT, cT.n)
                    sl = slice(512 * ts, 512 * (ts + 1))
                    if nm == "v":
                        self.q("vector", "tensor_copy", out=yb[:, sl],
                               in_=bank(bk), incs=[(cV, 1)])
                        self.dr("vector")
                    else:
                        self.q("vector", "tensor_scalar", out=yb[:, sl],
                               in0=bank(bk), scalar1=k_gamc[:, 0:1],
                               scalar2=k_betc[:, 0:1], op0=ALU.mult, op1=ALU.add,
                               incs=[(cV, 1)])
                        self.dr("vector")
                    self.bank_free[bk] = (cV, cV.n)
                    v_y.append(cV.n)
                    if nm != "v" and not self.fm_off:
                        self.w("scalar", cV, cV.n)
                        self.q("scalar", "activation",
                               out=s2f_[ts % 2][:],
                               in_=yb[:, sl], func=AF.Square,
                               incs=[(cS, 1)])
                        s_sq.append(cS.n)
                        if ts >= 1:
                            stat_pair(ts - 1)
                if nm != "v" and not self.fm_off:
                    stat_pair(NSL - 1)
                    self.w("vector", cT, cT.n)
                    for ts in range(min(NSL, 3)):
                        self.q("vector", "tensor_copy",
                               out=syr[32 * ts:32 * ts + 1, :],
                               in_=psF[32 * ts:32 * ts + 1, 3072:3584])
                        self.dr("vector")
                        self.q("vector", "tensor_copy",
                               out=sy2r[32 * ts:32 * ts + 1, :],
                               in_=psF[32 * ts:32 * ts + 1, 2560:3072],
                               incs=[(cV, 1)] if (NSL < 4 and ts == NSL - 1)
                               else ())
                        self.dr("vector")
                    if NSL == 4:
                        self.q("vector", "tensor_copy", out=syr[96:97, :],
                               in_=psF[0:1, 2048:2560])
                        self.dr("vector")
                        self.q("vector", "tensor_copy", out=sy2r[96:97, :],
                               in_=psF[0:1, 1024:1536], incs=[(cV, 1)])
                        self.dr("vector")
                        self.bank_free[4] = (cV, cV.n)
                        self.bank_free[2] = (cV, cV.n)
                    self.bank_free[5] = (cV, cV.n)
                    self.bank_free[6] = (cV, cV.n)

            def row_math(nm):
                self.q("vector", "tensor_scalar", out=mur[:, :],
                       in0=syr[:, :], scalar1=1.0 / 128.0, scalar2=None,
                       op0=ALU.mult)
                self.dr("vector")
                self.q("vector", "tensor_scalar", out=rsr[:, :],
                       in0=sy2r[:, :], scalar1=1.0 / 128.0, scalar2=None,
                       op0=ALU.mult)
                self.dr("vector")
                self.q("vector", "tensor_tensor", out=syr[:, :],
                       in0=mur[:, :], in1=mur[:, :], op=ALU.mult)
                self.dr("vector")
                self.q("vector", "scalar_tensor_tensor", out=rsr[:, :],
                       in0=syr[:, :], scalar=-1.0, in1=rsr[:, :],
                       op0=ALU.mult, op1=ALU.add)
                self.dr("vector")
                self.q("vector", "tensor_scalar", out=rsr[:, :],
                       in0=rsr[:, :], scalar1=EPS, scalar2=None, op0=ALU.add)
                self.dr("vector")
                self.q("vector", "tensor_copy", out=syr[:, :], in_=rsr[:, :])
                self.dr("vector")
                self.q("vector", "reciprocal", out=rsr[:, :],
                       in_=rsr[:, :], incs=[(cV, 1)])
                self.dr("vector")
                self.w("scalar", cV, cV.n)
                self.q("scalar", "activation", out=rsr[:, :],
                       in_=rsr[:, :], func=AF.Sqrt, incs=[(cS, 1)])
                self.w("vector", cS, cS.n)
                self.q("vector", "tensor_tensor", out=sy2r[:, :], in0=rsr[:, :],
                       in1=rsr[:, :], op=ALU.mult)
                self.dr("vector")
                self.q("vector", "tensor_tensor", out=sy2r[:, :], in0=sy2r[:, :],
                       in1=syr[:, :], op=ALU.mult)
                self.dr("vector")
                self.q("vector", "tensor_scalar", out=sy2r[:, :], in0=sy2r[:, :],
                       scalar1=-0.5, scalar2=1.5, op0=ALU.mult, op1=ALU.add)
                self.dr("vector")
                self.q("vector", "tensor_tensor", out=rsr[:, :], in0=rsr[:, :],
                       in1=sy2r[:, :], op=ALU.mult)
                self.dr("vector")
                if nm == "q":
                    self.q("vector", "tensor_scalar", out=rsr[:, :],
                           in0=rsr[:, :], scalar1=kRT2[:, 0:1], scalar2=None,
                           op0=ALU.mult)
                    self.dr("vector")
                if NSL == 4:
                    self.q("vector", "tensor_copy", out=mu0[0:1, :],
                           in_=mur[96:97, :])
                    self.dr("vector")
                    self.q("vector", "tensor_copy", out=rs0[0:1, :],
                           in_=rsr[96:97, :])
                    self.dr("vector")
                self.q("vector", "nop", incs=[(cV, 1)])
                return cV.n

            def fm_apply(nm, v_stats):
                dstT = qT if nm == "q" else kT
                yb = yb_[nm]
                for ts in range(NSL):
                    bmu, brs = (2, 3) if ts % 2 == 0 else (4, 5)
                    self.w("tensor", cV, v_stats)
                    mm_bank_wait(bmu)
                    mm_bank_wait(brs)
                    if ts < 3:
                        rhs_mu = mur[32 * ts:32 * ts + 1, :]
                        rhs_rs = rsr[32 * ts:32 * ts + 1, :]
                        ones = k_1all[32 * ts:32 * ts + 1, :]
                    else:
                        rhs_mu = mu0[0:1, :]
                        rhs_rs = rs0[0:1, :]
                        ones = None
                    pe_bcast(bank(bmu), rhs_mu, ones)
                    pe_bcast(bank(brs), rhs_rs, ones)
                    self.w("vector", cT, cT.n)
                    sl = slice(512 * ts, 512 * (ts + 1))
                    self.q("vector", "tensor_tensor", out=yb[:, sl], in0=yb[:, sl],
                           in1=bank(bmu), op=ALU.subtract)
                    self.dr("vector")
                    self.q("vector", "tensor_tensor", out=yb[:, sl], in0=yb[:, sl],
                           in1=bank(brs), op=ALU.mult, incs=[(cV, 1)])
                    self.dr("vector")
                    self.bank_free[bmu] = (cV, cV.n)
                    self.bank_free[brs] = (cV, cV.n)
                    self.q("vector", "tensor_copy", out=dstT[0][:, sl],
                           in_=yb[:, sl])
                    self.dr("vector")
                    self.q("vector", "scalar_tensor_tensor", out=dstT[1][:, sl],
                           in0=dstT[0][:, sl], scalar=-1.0, in1=yb[:, sl],
                           op0=ALU.mult, op1=ALU.add, incs=[(cV, 1)])
                    self.dr("vector")

            def v_transposes():
                yb = yb_["v"]
                v_done = {}
                for t in range(NT):
                    r = t % 4
                    reg = psF[:, 1024 + 512 * r:1024 + 512 * r + 128]
                    if t >= 4:
                        self.w("tensor", cV, v_done[t - 4])
                    else:
                        mm_bank_wait(2 + r)
                    self.q("tensor", "transpose", reg,
                           yb[:, 128 * t:128 * (t + 1)], k_idf[:],
                           incs=[(cT, 1)])
                    self.w("vector", cT, cT.n)
                    self.q("vector", "tensor_copy",
                           out=vB[0][:, 128 * t:128 * (t + 1)], in_=reg)
                    self.dr("vector")
                    self.q("vector", "scalar_tensor_tensor",
                           out=vB[1][:, 128 * t:128 * (t + 1)],
                           in0=vB[0][:, 128 * t:128 * (t + 1)], scalar=-1.0,
                           in1=reg, op0=ALU.mult, op1=ALU.add,
                           incs=[(cV, 1)])
                    self.dr("vector")
                    v_done[t] = cV.n
                for r in range(4):
                    last_t = (NT - 4) + r if NT >= 4 else r
                    if last_t < NT:
                        self.bank_free[2 + r] = (cV, v_done[last_t])

            self._fl_split = {0: 0, 1: 0}
            self._fl_avread = {0: 0, 1: 0}
            self.wdma("vector", "Lm4")
            for h in range(self.n_heads):
                load_slabs(0, "q", h)
                proj_mm(0, "q")
                load_slabs(1, "k", h)
                v_rm_q = 0 if self.fm_off else row_math("q")
                proj_mm(1, "k")
                if not self.fm_off:
                    fm_apply("q", v_rm_q)
                load_slabs(0, "v", h)
                v_rm_k = 0 if self.fm_off else row_math("k")
                proj_mm(0, "v")
                if not self.fm_off:
                    fm_apply("k", v_rm_k)
                v_head_ready = cV.n
                if not self.vt_off:
                    v_transposes()

                for qs in range(min(NQS, self.nqs_cap)):
                    nkb = 4 * (qs + 1)
                    q0 = 512 * qs
                    t_qk = {}
                    t_av = {}
                    s_sq = {}
                    v_split = {}

                    def emit_qk(kb):
                        bkS = 4 + (kb % 2)
                        mm_bank_wait(bkS)
                        if qs == 0 and kb == 0:
                            self.w("tensor", cV, v_head_ready)
                        self.q("tensor", "matmul", bank(bkS),
                               kT[0][:, 128 * kb:128 * (kb + 1)],
                               qT[0][:, q0:q0 + 512], start=True, stop=False)
                        self.q("tensor", "matmul", bank(bkS),
                               kT[0][:, 128 * kb:128 * (kb + 1)],
                               qT[1][:, q0:q0 + 512], start=False, stop=False)
                        self.q("tensor", "matmul", bank(bkS),
                               kT[1][:, 128 * kb:128 * (kb + 1)],
                               qT[0][:, q0:q0 + 512], start=False, stop=True,
                               incs=[(cT, 1)])
                        t_qk[kb] = cT.n

                    def emit_sq_split(kb):
                        bkS = 4 + (kb % 2)
                        pi = kb % 2
                        self.w("scalar", cT, t_qk[kb])
                        self.w("scalar", cV, self._fl_split[pi])
                        self.q("scalar", "activation", out=s2f_[pi][:],
                               in_=bank(bkS), func=AF.Square, incs=[(cS, 1)])
                        s_sq[kb] = cS.n
                        self.bank_free[bkS] = (cS, s_sq[kb])
                        self.w("vector", cS, s_sq[kb])
                        self.w("vector", cT, self._fl_avread[pi])
                        if pi == 0:
                            self.wdma("vector", "So")
                        if kb >= 4 * qs:
                            m = kb - 4 * qs
                            self.q("vector", "tensor_tensor", out=s2f_[pi][:],
                                   in0=s2f_[pi][:],
                                   in1=k_m4[:, 384 - 128 * m:896 - 128 * m],
                                   op=ALU.mult)
                            self.dr("vector")
                        self.q("vector", "tensor_copy", out=s2h_[pi][:],
                               in_=s2f_[pi][:])
                        self.dr("vector")
                        self.q("vector", "scalar_tensor_tensor", out=s2l_[pi][:],
                               in0=s2h_[pi][:], scalar=-1.0, in1=s2f_[pi][:],
                               op0=ALU.mult, op1=ALU.add, incs=[(cV, 1)])
                        self.dr("vector")
                        v_split[kb] = cV.n
                        self._fl_split[pi] = cV.n

                    def emit_av(kb):
                        pi = kb % 2
                        self.w("tensor", cV, v_split[kb])
                        if kb == 0:
                            mm_bank_wait(6)
                            mm_bank_wait(3)
                        self.q("tensor", "matmul", bank(6),
                               vB[0][:, 128 * kb:128 * (kb + 1)], s2h_[pi][:],
                               start=(kb == 0), stop=False)
                        self.q("tensor", "matmul", bank(6),
                               vB[1][:, 128 * kb:128 * (kb + 1)], s2h_[pi][:],
                               start=False, stop=False)
                        self.q("tensor", "matmul", bank(6),
                               vB[0][:, 128 * kb:128 * (kb + 1)], s2l_[pi][:],
                               start=False, stop=(kb == nkb - 1))
                        self.q("tensor", "matmul", bank(3)[0:1, :],
                               k_1cb[:], s2h_[pi][:], start=(kb == 0), stop=False)
                        self.q("tensor", "matmul", bank(3)[0:1, :],
                               k_1cb[:], s2l_[pi][:], start=False,
                               stop=(kb == nkb - 1), incs=[(cT, 1)])
                        t_av[kb] = cT.n
                        self._fl_avread[pi] = cT.n

                    emit_qk(0)
                    emit_sq_split(0)
                    for kb in range(1, nkb):
                        emit_qk(kb)
                        emit_sq_split(kb)
                        emit_av(kb - 1)
                    emit_av(nkb - 1)
                    tv_av = t_av[nkb - 1]
                    self.w("vector", cT, tv_av)
                    self.q("vector", "tensor_scalar", out=zrow[0:1, :],
                           in0=bank(3)[0:1, :], scalar1=EPS, scalar2=None, op0=ALU.add)
                    self.dr("vector")
                    self.q("vector", "reciprocal", out=zrow[0:1, :], in_=zrow[0:1, :],
                           incs=[(cV, 1)])
                    self.dr("vector")
                    self.w("tensor", cV, cV.n)
                    mm_bank_wait(2)
                    pe_bcast(bank(2), zrow[0:1, :])
                    self.w("vector", cT, cT.n)
                    self.q("vector", "tensor_copy", out=rzB, in_=bank(2),
                           incs=[(cV, 1)])
                    self.dr("vector")
                    self.bank_free[2] = (cV, cV.n)
                    self.q("vector", "tensor_tensor", out=ob, in0=bank(6),
                           in1=rzB, op=ALU.mult, incs=[(cV, 1)])
                    self.dr("vector")
                    self.bank_free[6] = (cV, cV.n)
                    self.bank_free[3] = (cV, cV.n)
                    self.wdma("vector", "So")
                    self.q("vector", "tensor_copy", out=obh[:], in_=ob)
                    self.dr("vector")
                    self.q("vector", "scalar_tensor_tensor", out=obl[:],
                           in0=obh[:], scalar=-1.0, in1=ob,
                           op0=ALU.mult, op1=ALU.add, incs=[(cV, 1)])
                    self.dr("vector")
                    self.w("scalar", cV, cV.n)
                    for tc in range(4):
                        t = 4 * qs + tc
                        self.dma("scalar",
                                 s_o[t][:, (2 * h) * 128:(2 * h) * 128 + 128],
                                 obh[:, 128 * tc:128 * (tc + 1)], "So")
                        self.dma("scalar",
                                 s_o[t][:, (2 * h + 1) * 128:(2 * h + 1) * 128 + 128],
                                 obl[:, 128 * tc:128 * (tc + 1)], "So")
            SH.close()
            SX.close()
            self.barrier()

            SW = contextlib.ExitStack()
            wo_ = [self.sb(SW, "wo_res%d" % i, (128, 2 * ND * 512), BF16)
                   for i in range(2)]
            ob_ = [self.sb(SW, "wo_o%d" % i, (128, H * 2 * 128), BF16)
                   for i in range(4)]
            xr2_ = [self.sb(SW, "wo_x%d" % i, (128, 512), F32) for i in range(2)]
            xo2_ = [self.sb(SW, "wo_y%d" % i, (128, 512), F32) for i in range(2)]
            self.wdma("sync", "Sspill0")
            self.wdma("sync", "Sspill1")
            self.wdma("sync", "So")
            v_ep = {}
            wo_used = [0, 0]

            def load_wo(ds):
                self.w("sync", cT, wo_used[ds % 2])
                wos = wo_[ds % 2]
                for hl in range(2):
                    for c in range(ND):
                        off = (hl * ND + c) * 512
                        self.dma("sync", wos[:, off:off + 512], s_woh[ds, hl, c],
                                 "Lwo%d" % (ds % 2))

            load_wo(0)
            for ds in range(4):
                wos = wo_[ds % 2]
                if ds + 1 < 4:
                    load_wo(ds + 1)
                for t in range(NT):
                    obt = ob_[t % 4]
                    nmo = "Lo%d" % (t % 4)
                    if t >= 2 or ds > 0:
                        pk = (ds, t - 2) if t >= 2 else (ds - 1, NT - 2 + t)
                        self.w("sync", cV, v_ep[pk])
                    self.dma("sync", obt[:], s_o[t], nmo)
                    self.dma("sync", xr2_[t % 2][:],
                             x_src[128 * t:128 * (t + 1),
                                   512 * ds:512 * (ds + 1)], "Lxr2%d" % (t % 2))
                    self.wdma("tensor", nmo)
                    self.wdma("tensor", "Lwo%d" % (ds % 2))
                    bk = t % 2
                    mm_bank_wait(bk)
                    for c in range(ND):
                        ohi = obt[:, (2 * c) * 128:(2 * c) * 128 + 128]
                        olo = obt[:, (2 * c + 1) * 128:(2 * c + 1) * 128 + 128]
                        whi = wos[:, (0 * ND + c) * 512:(0 * ND + c) * 512 + 512]
                        wlo = wos[:, (1 * ND + c) * 512:(1 * ND + c) * 512 + 512]
                        self.q("tensor", "matmul", bank(bk), ohi, whi,
                               start=(c == 0), stop=False)
                        self.q("tensor", "matmul", bank(bk), ohi, wlo,
                               start=False, stop=False)
                        self.q("tensor", "matmul", bank(bk), olo, whi,
                               start=False, stop=(c == ND - 1),
                               incs=[(cT, 1)] if c == ND - 1 else ())
                    if t == NT - 1:
                        wo_used[ds % 2] = cT.n
                    self.w("vector", cT, cT.n)
                    self.wdma("vector", "Lxr2%d" % (t % 2))
                    self.wdma("vector", "Sx2%d" % (t % 2))
                    self.q("vector", "tensor_tensor", out=xo2_[t % 2][:],
                           in0=bank(bk), in1=xr2_[t % 2][:], op=ALU.add,
                           incs=[(cV, 1)])
                    self.dr("vector")
                    self.bank_free[bk] = (cV, cV.n)
                    v_ep[(ds, t)] = cV.n
                    self.w("scalar", cV, cV.n)
                    self.dma("scalar",
                             x_next_dst[128 * t:128 * (t + 1), 512 * ds:512 * (ds + 1)],
                             xo2_[t % 2][:], "Sx2%d" % (t % 2))
            SW.close()
            ACALL.close()
            self.barrier()

        if "ffn1" in self.parts:
            ffn(x_in, s_x1, 0)
            cur = s_x1
        else:
            cur = x_in
        if "attn" in self.parts:
            attn(cur, s_x2)
            cur = s_x2
        if "ffn2" in self.parts:
            ffn(cur, y_out, 1)
        else:
            SCP = contextlib.ExitStack()
            tb = self.sb(SCP, "cp_t", (128, D), F32)
            for t in range(NT):
                self.dma("sync", tb[:], cur[128 * t:128 * (t + 1), :], "cp")
                self.wdma("sync", "cp")
                self.dma("sync", y_out[128 * t:128 * (t + 1), :], tb[:], "cp")
                self.wdma("sync", "cp")
            SCP.close()
        for nm2 in list(self.dsems):
            self.wdma("sync", nm2)

        with self.nc.Block() as block:
            @block.sync
            def _(e):
                self.replay("sync", e)

            @block.tensor
            def _(e):
                self.replay("tensor", e)

            @block.vector
            def _(e):
                self.replay("vector", e)

            @block.scalar
            def _(e):
                self.replay("scalar", e)

            @block.gpsimd
            def _(e):
                self.replay("gpsimd", e)
        self._P.close()
        self._sem_stack.close()
        return self.nc

    def replay(self, ename, e):
        for idx, item in enumerate(self.plan[ename]):
            k = item[0]
            if k == "call":
                _, fname, args, kw, incs = item
                try:
                    inst = getattr(e, fname)(*args, **kw)
                except Exception:
                    import sys
                    print("REPLAY FAIL %s[%d] %s args=%r kw=%r" %
                          (ename, idx, fname, args, kw), file=sys.stderr)
                    raise
                for ct, amt in incs:
                    inst = inst.then_inc(ct.h, amt)
            elif k == "wait":
                e.wait_ge(item[1].h, item[2])
            else:
                e.drain()


def host_inputs(inp, T):
    c = np.ascontiguousarray
    f32 = np.float32
    bf = ml_dtypes.bfloat16
    out = {
        "w1t": c(np.asarray(inp["L0_ffn_w1"], f32).T),
        "w2t": c(np.asarray(inp["L0_ffn_w2"], f32).T),
        "wqt": c(np.asarray(inp["L0_attn_Wq"], f32).T),
        "wkt": c(np.asarray(inp["L0_attn_Wk"], f32).T),
        "wvt": c(np.asarray(inp["L0_attn_Wv"], f32).T),
        "wot": c(np.asarray(inp["L0_attn_Wo"], f32).T),
        "b1r": c(np.asarray(inp["L0_ffn_b1"], f32).reshape(16, 512)),
        "b2r": c(np.asarray(inp["L0_ffn_b2"], f32).reshape(4, 512)),
        "gFr": c(np.asarray(inp["L0_ffn_gamma"], f32).reshape(16, 512)),
        "bFr": c(np.asarray(inp["L0_ffn_beta"], f32).reshape(16, 512)),
        "gamc": c(np.asarray(inp["L0_attn_gamma"], f32).reshape(128, 1)),
        "betc": c(np.asarray(inp["L0_attn_beta"], f32).reshape(128, 1)),
        "identf": np.eye(128, dtype=f32),
        "identb": np.eye(128, dtype=bf),
        "onesf": np.ones((1, 128), dtype=f32),
        "onescf": np.ones((128, 1), dtype=f32),
        "onescb": np.ones((128, 1), dtype=bf),
    }
    cc = np.arange(896)[None, :]
    pp = np.arange(128)[:, None]
    out["mask4"] = (cc >= pp + 384).astype(bf)
    return out


def kernel(**inputs):
    T = 2048
    x = np.ascontiguousarray(np.asarray(inputs["L0_attn_hidden_states"],
                                        dtype=np.float32))
    B = x.shape[0]
    shared = host_inputs(inputs, T)
    mk = MK(T, ("ffn1", "attn", "ffn2"))
    nc = mk.build()
    in_maps = []
    for b in range(B):
        m = dict(shared)
        m["x"] = np.ascontiguousarray(x[b])
        in_maps.append(m)
    res = run_bass_kernel_spmd(nc, in_maps, core_ids=list(range(B)))
    return np.stack([np.asarray(r["y"], dtype=np.float32) for r in res.results],
                    axis=0)


# version 24
# speedup vs baseline: 1.0483x; 1.0483x over previous
import contextlib
import numpy as np
import ml_dtypes
import concourse.bass as bass
import concourse.mybir as mybir
from concourse.bass_utils import run_bass_kernel_spmd

F32 = mybir.dt.float32
BF16 = mybir.dt.bfloat16
FP8 = mybir.dt.float8e4
I16 = mybir.dt.int16
AF = mybir.ActivationFunctionType
ALU = mybir.AluOpType
AX = mybir.AxisListType

D = 2048
FF = 8192
H = 16
ND = D // 128
NFS = FF // 512
NFC = FF // 128
EPS = 1e-5
RT2 = float(128.0 ** -0.5)

ENGINES = ("sync", "tensor", "vector", "scalar", "gpsimd")


class Ct:
    def __init__(self, h):
        self.h = h
        self.n = 0

    def inc(self, k=1):
        self.n += k
        return self.n


class MK:

    def __init__(self, T, parts):
        self.T = T
        self.NT = T // 128
        self.NQS = T // 512
        self.parts = parts
        self.nc = bass.Bass("TRN2", target_bir_lowering=False)
        self.plan = {e: [] for e in ENGINES}
        self.dsems = {}
        self.dbg = False
        self.n_heads = H
        self.nqs_cap = 99
        self.fm_off = False
        self.vt_off = False
        self.proj_off = False

    def q(self, eng, fname, *args, incs=(), **kw):
        for ct, amt in incs:
            ct.inc(amt)
        self.plan[eng].append(("call", fname, args, kw, tuple(incs)))

    def w(self, eng, ct, val):
        if val > 0:
            self.plan[eng].append(("wait", ct, val))

    def dr(self, eng):
        self.plan[eng].append(("drain",))

    def dsem(self, name):
        if name not in self.dsems:
            self.dsems[name] = Ct(self._sem_stack.enter_context(
                self.nc.semaphore("d_" + name)))
        return self.dsems[name]

    def dma(self, queue, out, in_, name):
        ct = self.dsem(name)
        self.q(queue, "dma_start", out=out, in_=in_, incs=[(ct, 16)])
        return ct.n

    def wdma(self, eng, name):
        ct = self.dsem(name)
        self.w(eng, ct, ct.n)

    def barrier(self):
        engs = ("tensor", "vector", "scalar", "gpsimd", "sync")
        for e in engs:
            self.q(e, "nop", incs=[(self.cB[e], 1)])
        cts = [self.cT, self.cV, self.cS, self.cG, self.cY]
        vals = ([(self.cB[e], self.cB[e].n) for e in engs]
                + [(c, c.n) for c in cts]
                + [(c, c.n) for c in self.dsems.values()])
        for e in engs:
            for c, v in vals:
                self.w(e, c, v)

    def sb(self, stack, name, shape, dt):
        return stack.enter_context(self.nc.sbuf_tensor(name, list(shape), dt))

    def build(self):
        nc = self.nc
        T, NT, NQS = self.T, self.NT, self.NQS

        def din(name, shape, dt=F32):
            return nc.dram_tensor(name, list(shape), dt, kind="ExternalInput").ap()

        x_in = din("x", (T, D))
        w1t = din("w1t", (D, FF))
        w2t = din("w2t", (FF, D))
        wt = {nm: din("w%st" % nm, (D, D)) for nm in ("q", "k", "v", "o")}
        b1r = din("b1r", (16, 512))
        b2r = din("b2r", (4, 512))
        gFr = din("gFr", (16, 512))
        bFr = din("bFr", (16, 512))
        gamc = din("gamc", (128, 1))
        betc = din("betc", (128, 1))
        identf = din("identf", (128, 128))
        identb = din("identb", (128, 128), BF16)
        onesf = din("onesf", (1, 128))
        onescf = din("onescf", (128, 1))
        onescb = din("onescb", (128, 1), BF16)
        mask4 = din("mask4", (128, 896), BF16)

        y_out = nc.dram_tensor("y", [T, D], F32, kind="ExternalOutput").ap()

        def scr(name, shape, dt):
            return nc.dram_tensor(name, list(shape), dt, kind="Internal").ap()

        s_w1q = scr("s_w1q", (D, FF), FP8)
        s_w2q = scr("s_w2q", (128, NFC * D), FP8)
        s_g = scr("s_g", (T, FF), F32)
        s_hq = scr("s_hq", (NT, 128, NFC * 128), BF16)
        s_x1 = scr("s_x1", (T, D), F32)
        s_x2 = scr("s_x2", (T, D), F32)
        s_o = scr("s_o", (NT, 128, H * 2 * 128), BF16)
        s_wh = {nm: scr("s_w%sh" % nm, (H, 2, ND, 128, 128), BF16)
                for nm in ("q", "k", "v")}
        s_woh = scr("s_woh", (4, 2, ND, 128, 512), BF16)
        s_dbg = (nc.dram_tensor("dbg", [16, 128, 512], F32, kind="ExternalOutput").ap()
                 if self.dbg else None)

        sem_stack = contextlib.ExitStack()
        self._sem_stack = sem_stack
        mk_sem = lambda nm: sem_stack.enter_context(nc.semaphore(nm))
        self.dL = Ct(mk_sem("dL"))
        self.dS = Ct(mk_sem("dS"))
        self.cT = Ct(mk_sem("cT"))
        self.cV = Ct(mk_sem("cV"))
        self.cS = Ct(mk_sem("cS"))
        self.cG = Ct(mk_sem("cG"))
        self.cY = Ct(mk_sem("cY"))
        self.cB = {e: Ct(mk_sem("cB_" + e))
                   for e in ("tensor", "vector", "scalar", "gpsimd", "sync")}
        dL, dS, cT, cV, cS = self.dL, self.dS, self.cT, self.cV, self.cS

        P = contextlib.ExitStack()
        self._P = P
        k_idf = self.sb(P, "k_idf", (128, 128), F32)
        k_idb = self.sb(P, "k_idb", (128, 128), BF16)
        k_1all = self.sb(P, "k_1all", (128, 128), F32)
        k_1cf = self.sb(P, "k_1cf", (128, 1), F32)
        k_1cb = self.sb(P, "k_1cb", (128, 1), BF16)
        k_gamc = self.sb(P, "k_gamc", (128, 1), F32)
        k_betc = self.sb(P, "k_betc", (128, 1), F32)
        st_t = self.sb(P, "st_t", (128, 16), F32)
        st_bn = self.sb(P, "st_bn", (128, 24), F32)
        st_sc = self.sb(P, "st_sc", (128, 4), F32)
        k127i = self.sb(P, "k127i", (128, 1), F32)
        kRT2 = self.sb(P, "kRT2", (128, 1), F32)

        psF = P.enter_context(nc.psum_tensor("psF", [128, 3584], F32))
        psH = P.enter_context(nc.psum_tensor("psH", [128, 1024], BF16))

        def bank(i, wdt=512):
            return psF[:, 512 * i:512 * i + wdt]

        self.bank_free = {}

        def mm_bank_wait(b):
            if b in self.bank_free:
                c, v = self.bank_free[b]
                self.w("tensor", c, v)

        def pe_bcast(psum_ap, row_ap, ones=None):
            self.q("tensor", "matmul", psum_ap,
                   k_1all[0:1, :] if ones is None else ones, row_ap,
                   start=True, stop=True, incs=[(cT, 1)])
            return cT.n

        def ln_stats_tile(xb, mu_dst, rstd_dst):
            for c in range(4):
                self.q("vector", "bn_stats", out=st_bn[:, 6 * c:6 * (c + 1)],
                       in_=xb[:, 512 * c:512 * (c + 1)])
            self.dr("vector")
            self.q("vector", "bn_aggr", out=st_t[:, 0:2], in_=st_bn[:, 0:24])
            self.dr("vector")
            self.q("vector", "tensor_copy", out=mu_dst, in_=st_t[:, 0:1])
            self.dr("vector")
            self.q("vector", "tensor_scalar", out=st_t[:, 2:3], in0=st_t[:, 1:2],
                   scalar1=EPS, scalar2=None, op0=ALU.add)
            self.dr("vector")
            self.q("vector", "reciprocal", out=st_t[:, 3:4], in_=st_t[:, 2:3],
                   incs=[(cV, 1)])
            self.dr("vector")
            self.w("scalar", cV, cV.n)
            self.q("scalar", "activation", out=st_t[:, 10:11], in_=st_t[:, 3:4],
                   func=AF.Sqrt, incs=[(cS, 1)])
            self.w("vector", cS, cS.n)
            self.q("vector", "tensor_scalar", out=st_t[:, 11:12], in0=st_t[:, 10:11],
                   scalar1=st_t[:, 10:11], scalar2=None, op0=ALU.mult)
            self.dr("vector")
            self.q("vector", "tensor_scalar", out=st_t[:, 11:12], in0=st_t[:, 11:12],
                   scalar1=st_t[:, 2:3], scalar2=None, op0=ALU.mult)
            self.dr("vector")
            self.q("vector", "tensor_scalar", out=st_t[:, 11:12], in0=st_t[:, 11:12],
                   scalar1=-0.5, scalar2=1.5, op0=ALU.mult, op1=ALU.add)
            self.dr("vector")
            self.q("vector", "tensor_scalar", out=rstd_dst, in0=st_t[:, 11:12],
                   scalar1=st_t[:, 10:11], scalar2=None, op0=ALU.mult)
            self.dr("vector")

        def round_chain(buf, r0buf, out_ap):
            self.w("scalar", cV, cV.n)
            self.q("scalar", "copy", out=r0buf, in_=buf, incs=[(cS, 1)])
            self.w("vector", cS, cS.n)
            self.q("vector", "tensor_tensor", out=buf, in0=buf, in1=r0buf,
                   op=ALU.subtract)
            self.dr("vector")
            self.q("vector", "tensor_scalar", out=buf, in0=buf,
                   scalar1=0.5, scalar2=None, op0=ALU.is_ge)
            self.dr("vector")
            self.q("vector", "scalar_tensor_tensor", out=out_ap, in0=buf,
                   scalar=-512.0, in1=r0buf, op0=ALU.add, op1=ALU.add,
                   incs=[(cV, 1)])
            self.dr("vector")
            return cV.n

        def round_wide(buf, r0buf, wbuf, cbuf, out_ap, nhalf=1, idle_cb=None):
            self.w("scalar", cV, cV.n)
            self.q("scalar", "copy", out=r0buf, in_=buf, incs=[(cS, 1)])
            self.w("vector", cS, cS.n)
            n = buf.shape[1]
            hw = n // nhalf
            for hh2 in range(nhalf):
                sl = slice(hh2 * hw, (hh2 + 1) * hw)
                self.q("vector", "scalar_tensor_tensor", out=wbuf[:, 0:hw],
                       in0=buf[:, sl], scalar=16.5, in1=r0buf[:, sl],
                       op0=ALU.add, op1=ALU.subtract, incs=[(cV, 1)])
                self.dr("vector")
                for b in (16.0, 8.0, 4.0, 2.0, 1.0):
                    self.q("vector", "tensor_scalar", out=cbuf[:, 0:hw],
                           in0=wbuf[:, 0:hw], scalar1=b, scalar2=-b,
                           op0=ALU.is_ge, op1=ALU.mult, incs=[(cV, 1)])
                    self.dr("vector")
                    self.w("gpsimd", cV, cV.n)
                    self.q("gpsimd", "tensor_tensor", out=wbuf[:, 0:hw],
                           in0=wbuf[:, 0:hw], in1=cbuf[:, 0:hw],
                           op=ALU.add, incs=[(self.cG, 1)])
                    self.dr("gpsimd")
                    if idle_cb is not None:
                        idle_cb()
                    self.w("vector", self.cG, self.cG.n)
                self.q("vector", "scalar_tensor_tensor", out=out_ap[:, sl] if nhalf > 1 else out_ap,
                       in0=buf[:, sl], scalar=-511.5, in1=wbuf[:, 0:hw],
                       op0=ALU.add, op1=ALU.subtract, incs=[(cV, 1)])
                self.dr("vector")
            return cV.n

        for dst, srca in ((k_idf, identf), (k_idb, identb),
                          (k_1cf, onescf), (k_1cb, onescb),
                          (k_gamc, gamc), (k_betc, betc)):
            self.dma("sync", dst[:], srca[:], "c0")
        self.wdma("tensor", "c0")
        self.wdma("vector", "c0")
        self.wdma("scalar", "c0")

        self.q("vector", "memset", k_1all[:], 1.0)
        self.dr("vector")
        self.q("vector", "memset", k127i[:], 127.0)
        self.dr("vector")
        self.q("vector", "reciprocal", out=k127i[:], in_=k127i[:])
        self.dr("vector")
        self.q("vector", "memset", kRT2[:], 128.0)
        self.dr("vector")
        self.q("vector", "reciprocal", out=kRT2[:], in_=kRT2[:], incs=[(cV, 1)])
        self.dr("vector")
        self.w("scalar", cV, cV.n)
        self.q("scalar", "activation", out=st_t[:, 10:11], in_=kRT2[:],
               func=AF.Sqrt, incs=[(cS, 1)])
        self.w("vector", cS, cS.n)
        self.q("vector", "tensor_scalar", out=st_t[:, 11:12], in0=st_t[:, 10:11],
               scalar1=st_t[:, 10:11], scalar2=128.0, op0=ALU.mult, op1=ALU.mult)
        self.dr("vector")
        self.q("vector", "tensor_scalar", out=st_t[:, 11:12], in0=st_t[:, 11:12],
               scalar1=-0.5, scalar2=1.5, op0=ALU.mult, op1=ALU.add)
        self.dr("vector")
        self.q("vector", "tensor_scalar", out=kRT2[:], in0=st_t[:, 11:12],
               scalar1=st_t[:, 10:11], scalar2=None, op0=ALU.mult)
        self.dr("vector")

        def weight_mean_quant(wt_ap, s_out, nchunks, width, sccol, tag,
                              lay=False):
            WS = contextlib.ExitStack()
            a = self.sb(WS, "wq_a_%s" % tag, (128, width), F32)
            b = self.sb(WS, "wq_b_%s" % tag, (128, width), F32)
            r0 = self.sb(WS, "wq_r_%s" % tag, (128, width), I16)
            q8 = self.sb(WS, "wq_q_%s" % tag, (128, width), FP8)
            nsl_w = width // 512
            acc = self.sb(WS, "wq_s_%s" % tag, (128, nchunks * nsl_w), F32)
            bufs = [a, b]
            vload = []
            for c in range(nchunks):
                bb = bufs[c % 2]
                nmL = "Lw%s%d" % (tag, c % 2)
                if c >= 2:
                    self.w("sync", cS, vload[c - 2][1])
                self.dma("sync", bb[:], wt_ap[128 * c:128 * (c + 1), :], nmL)
                self.wdma("scalar", nmL)
                for sl2 in range(nsl_w):
                    self.q("scalar", "activation", out=bb[:, 512 * sl2:512 * (sl2 + 1)],
                           in_=bb[:, 512 * sl2:512 * (sl2 + 1)], func=AF.Abs,
                           accum_out=acc[:, c * nsl_w + sl2:c * nsl_w + sl2 + 1],
                           incs=[(cS, 1)] if sl2 == nsl_w - 1 else ())
                vload.append(("s", self.cS.n))
            self.w("vector", cS, self.cS.n)
            self.q("vector", "tensor_reduce", out=st_t[:, 0:1],
                   in_=acc[:, 0:nchunks * nsl_w], op=ALU.add, axis=AX.X,
                   incs=[(cV, 1)])
            self.dr("vector")
            self.w("tensor", cV, cV.n)
            mm_bank_wait(6)
            self.q("tensor", "matmul", bank(6, 1)[0:1, :], k_1cf[:], st_t[:, 0:1],
                   start=True, stop=True, incs=[(cT, 1)])
            self.w("vector", cT, cT.n)
            self.q("vector", "tensor_scalar", out=st_t[0:1, 1:2], in0=bank(6, 1)[0:1, 0:1],
                   scalar1=1.0 / (D * FF), scalar2=EPS, op0=ALU.mult, op1=ALU.max)
            self.dr("vector")
            self.q("vector", "reciprocal", out=st_t[0:1, 2:3], in_=st_t[0:1, 1:2])
            self.dr("vector")
            self.q("vector", "nop", incs=[(cV, 1)])
            self.w("tensor", cV, cV.n)
            self.q("tensor", "matmul", bank(6, 2), k_1all[0:1, :], st_t[0:1, 1:3],
                   start=True, stop=True, incs=[(cT, 1)])
            self.w("vector", cT, cT.n)
            self.q("vector", "tensor_copy", out=st_sc[:, sccol:sccol + 2],
                   in_=bank(6, 2), incs=[(cV, 1)])
            self.dr("vector")
            self.bank_free[6] = (cV, cV.n)
            v_sc = cV.n
            vclip = []
            for c in range(nchunks):
                bb = bufs[c % 2]
                nmL = "Lw%s%d" % (tag, c % 2)
                if c >= 2:
                    self.w("sync", cV, vclip[c - 2])
                else:
                    self.w("sync", cS, vload[nchunks - 2 + c][1])
                self.dma("sync", bb[:], wt_ap[128 * c:128 * (c + 1), :], nmL)
                self.wdma("vector", nmL)
                self.w("vector", cV, v_sc)
                self.q("vector", "tensor_scalar", out=bb[:], in0=bb[:],
                       scalar1=st_sc[:, sccol + 1:sccol + 2], scalar2=512.0,
                       op0=ALU.mult, op1=ALU.add, incs=[(cV, 1)])
                self.dr("vector")
                round_chain(bb[:], r0[:], bb[:])
                self.wdma("vector", "Sw" + tag)
                self.q("vector", "tensor_scalar", out=q8[:], in0=bb[:],
                       scalar1=1.0, scalar2=-1.0, op0=ALU.min, op1=ALU.max,
                       incs=[(cV, 1)])
                self.dr("vector")
                vclip.append(cV.n)
                self.w("scalar", cV, cV.n)
                if lay:
                    self.dma("scalar", s_out[:, c * width:(c + 1) * width],
                             q8[:], "Sw" + tag)
                else:
                    self.dma("scalar", s_out[128 * c:128 * (c + 1), :], q8[:],
                             "Sw" + tag)
            WS.close()
            self.barrier()
            return v_sc, None

        def mk_store(nm):
            def store(c, hi, lo, sem):
                for h in range(H):
                    self.dma("gpsimd", s_wh[nm][h, 0, c],
                             hi[:, 128 * h:128 * (h + 1)], sem)
                    self.dma("gpsimd", s_wh[nm][h, 1, c],
                             lo[:, 128 * h:128 * (h + 1)], sem)
            return store

        def store_o(c, hi, lo, sem):
            for ds in range(4):
                self.dma("gpsimd", s_woh[ds, 0, c],
                         hi[:, 512 * ds:512 * (ds + 1)], sem)
                self.dma("gpsimd", s_woh[ds, 1, c],
                         lo[:, 512 * ds:512 * (ds + 1)], sem)

        split_specs = [(wt["q"], "q", mk_store("q")), (wt["k"], "k", mk_store("k")),
                       (wt["v"], "v", mk_store("v")), (wt["o"], "o", store_o)]

        def make_split_emitters(a, b, hi_, lo_):
            st = {"v": [], "spm": []}
            bufs = [a, b]
            flat = [(wt_ap, tag, store_fn, c)
                    for wt_ap, tag, store_fn in split_specs
                    for c in range(ND)]
            n = len(flat)

            def load(cc):
                wt_ap, _, _, c = flat[cc]
                if cc >= 2:
                    self.w("gpsimd", cV, st["v"][cc - 2])
                self.dma("gpsimd", bufs[cc % 2][:],
                         wt_ap[128 * c:128 * (c + 1), :], "Lsp%d" % (cc % 2))

            def em(cc):
                if cc == 0:
                    load(0)
                if cc + 1 < n:
                    load(cc + 1)
                _, tag, store_fn, c = flat[cc]
                bb = bufs[cc % 2]
                hi = hi_[cc % 2]
                lo = lo_[cc % 2]
                self.wdma("vector", "Lsp%d" % (cc % 2))
                self.wdma("vector", "Sspill%d" % (cc % 2))
                self.q("vector", "tensor_copy", out=hi[:], in_=bb[:])
                self.dr("vector")
                self.q("vector", "scalar_tensor_tensor", out=lo[:],
                       in0=hi[:], scalar=-1.0, in1=bb[:],
                       op0=ALU.mult, op1=ALU.add, incs=[(cV, 1)])
                self.dr("vector")
                st["v"].append(cV.n)
                self.w("gpsimd", cV, cV.n)
                store_fn(c, hi, lo, "Sspill%d" % (cc % 2))
            return [(lambda cc=cc: em(cc)) for cc in range(n)]

        def make_w2q_emitters(a, b, r0, q8, acc):
            nsl_w = D // 512
            st = {"vload": [], "vclip": [], "vsc": 0}
            bufs = [a, b]
            ems = []

            def load1(c):
                if c >= 2:
                    self.w("sync", cS, st["vload"][c - 2])
                self.dma("sync", bufs[c % 2][:],
                         w2t[128 * c:128 * (c + 1), :], "Lww2%d" % (c % 2))

            for c in range(NFC):
                def em(c=c):
                    if c == 0:
                        load1(0)
                    if c + 1 < NFC:
                        load1(c + 1)
                    bb = bufs[c % 2]
                    self.wdma("scalar", "Lww2%d" % (c % 2))
                    for sl2 in range(nsl_w):
                        self.q("scalar", "activation",
                               out=bb[:, 512 * sl2:512 * (sl2 + 1)],
                               in_=bb[:, 512 * sl2:512 * (sl2 + 1)], func=AF.Abs,
                               accum_out=acc[:, c * nsl_w + sl2:c * nsl_w + sl2 + 1],
                               incs=[(cS, 1)] if sl2 == nsl_w - 1 else ())
                    st["vload"].append(cS.n)
                ems.append(em)

            def fin():
                self.w("vector", cS, st["vload"][-1])
                self.q("vector", "tensor_reduce", out=st_bn[:, 0:1],
                       in_=acc[:, 0:NFC * nsl_w], op=ALU.add, axis=AX.X,
                       incs=[(cV, 1)])
                self.dr("vector")
                self.w("tensor", cV, cV.n)
                mm_bank_wait(6)
                self.q("tensor", "matmul", bank(6, 1)[0:1, :], k_1cf[:],
                       st_bn[:, 0:1], start=True, stop=True, incs=[(cT, 1)])
                self.w("vector", cT, cT.n)
                self.q("vector", "tensor_scalar", out=st_bn[0:1, 1:2],
                       in0=bank(6, 1)[0:1, 0:1], scalar1=1.0 / (D * FF),
                       scalar2=EPS, op0=ALU.mult, op1=ALU.max)
                self.dr("vector")
                self.q("vector", "reciprocal", out=st_bn[0:1, 2:3],
                       in_=st_bn[0:1, 1:2])
                self.dr("vector")
                self.q("vector", "nop", incs=[(cV, 1)])
                self.w("tensor", cV, cV.n)
                self.q("tensor", "matmul", bank(6, 2), k_1all[0:1, :],
                       st_bn[0:1, 1:3], start=True, stop=True, incs=[(cT, 1)])
                self.w("vector", cT, cT.n)
                self.q("vector", "tensor_copy", out=st_sc[:, 2:4],
                       in_=bank(6, 2), incs=[(cV, 1)])
                self.dr("vector")
                self.bank_free[6] = (cV, cV.n)
                st["vsc"] = cV.n
            ems.append(fin)

            def load2(c):
                if c >= 2:
                    self.w("sync", cV, st["vclip"][c - 2])
                else:
                    self.w("sync", cS, st["vload"][NFC - 2 + c])
                self.dma("sync", bufs[c % 2][:],
                         w2t[128 * c:128 * (c + 1), :], "Lww2%d" % (c % 2))

            for c in range(NFC):
                def em(c=c):
                    if c == 0:
                        load2(0)
                    if c + 1 < NFC:
                        load2(c + 1)
                    bb = bufs[c % 2]
                    self.wdma("vector", "Lww2%d" % (c % 2))
                    self.w("vector", cV, st["vsc"])
                    self.q("vector", "tensor_scalar", out=bb[:], in0=bb[:],
                           scalar1=st_sc[:, 3:4], scalar2=512.0,
                           op0=ALU.mult, op1=ALU.add, incs=[(cV, 1)])
                    self.dr("vector")
                    round_chain(bb[:], r0[:], bb[:])
                    self.wdma("vector", "Sww2")
                    self.q("vector", "tensor_scalar", out=q8[:], in0=bb[:],
                           scalar1=1.0, scalar2=-1.0, op0=ALU.min, op1=ALU.max,
                           incs=[(cV, 1)])
                    self.dr("vector")
                    st["vclip"].append(cV.n)
                    self.w("scalar", cV, cV.n)
                    self.dma("scalar", s_w2q[:, c * D:(c + 1) * D], q8[:],
                             "Sww2")
                ems.append(em)
            return ems

        v_sc1, _ = weight_mean_quant(w1t, s_w1q, ND, FF, 0, "w1")
        if "ffn1" not in self.parts:
            if "ffn2" in self.parts:
                weight_mean_quant(w2t, s_w2q, NFC, D, 2, "w2", lay=True)
            if "attn" in self.parts:
                WS = contextlib.ExitStack()
                a = self.sb(WS, "ws_a", (128, D), F32)
                b = self.sb(WS, "ws_b", (128, D), F32)
                hi = [self.sb(WS, "ws_h%d" % i, (128, D), BF16)
                      for i in range(2)]
                lo = [self.sb(WS, "ws_l%d" % i, (128, D), BF16)
                      for i in range(2)]
                for em in make_split_emitters(a, b, hi, lo):
                    em()
                WS.close()
                self.barrier()

        def ffn(x_src, x_next_dst, cid):
            tg = "f%d" % cid
            prep = (cid == 0 and "ffn1" in self.parts)
            CALL = contextlib.ExitStack()
            k_b2B = self.sb(CALL, "k_b2B%s" % tg, (128, 2048), F32)
            rstg = [self.sb(CALL, "rstg%d%s" % (i, tg), (1, 512), F32)
                    for i in range(2)]
            st_mu = self.sb(CALL, "st_mu%s" % tg, (128, NT), F32)
            st_rstd = self.sb(CALL, "st_rstd%s" % tg, (128, NT), F32)
            st_m2 = self.sb(CALL, "st_m2%s" % tg, (128, NT), F32)
            st_sv = self.sb(CALL, "st_sv%s" % tg, (128, NT), F32)
            st_sv2 = self.sb(CALL, "st_sv2%s" % tg, (128, NT), F32)

            last_bc = {}

            def bcast_row(dram_row, bk_i, stg_i, wdt=512):
                nmr = "Lr%s%d" % (tg, stg_i)
                if stg_i in last_bc:
                    self.w("sync", cT, last_bc[stg_i])
                self.dma("sync", rstg[stg_i][0:1, :wdt], dram_row, nmr)
                self.wdma("tensor", nmr)
                mm_bank_wait(bk_i)
                v = pe_bcast(bank(bk_i, wdt), rstg[stg_i][0:1, :wdt])
                last_bc[stg_i] = v
                return v

            for j in range(4):
                bcast_row(b2r[j:j + 1, :], 2 + j, j % 2)
            tvb = cT.n
            self.w("vector", cT, tvb)
            for j in range(4):
                self.q("vector", "tensor_scalar", out=k_b2B[:, 512 * j:512 * (j + 1)],
                       in0=bank(2 + j), scalar1=0.5, scalar2=None, op0=ALU.mult,
                       incs=[(cV, 1)] if j == 3 else ())
                self.dr("vector")
            for j in range(4):
                self.bank_free[2 + j] = (cV, cV.n)

            HB = contextlib.ExitStack()
            hbn = self.sb(HB, "hbn%s" % tg, (128, 6 * NFS * NT), F32)

            SX = contextlib.ExitStack()
            xqT = self.sb(SX, "xqT%s" % tg, (128, ND * T), BF16)
            xqT3 = xqT[:].rearrange("p (c t2) -> p c t2", c=ND)
            SA = contextlib.ExitStack()
            xb_ = [self.sb(SA, "fa_x%d%s" % (i, tg), (128, D), F32) for i in range(2)]
            db = self.sb(SA, "fa_d%s" % tg, (128, D), F32)
            ub = self.sb(SA, "fa_u%s" % tg, (128, D), F32)
            r0b = self.sb(SA, "fa_r%s" % tg, (128, D), I16)
            iqb = self.sb(SA, "fa_q%s" % tg, (128, D), BF16)
            SP = contextlib.ExitStack()
            il_mm1 = ()
            if prep:
                sp_a = self.sb(SP, "sp_a", (128, D), F32)
                sp_b = self.sb(SP, "sp_b", (128, D), F32)
                sp_h = [self.sb(SP, "sp_h%d" % i, (128, D), BF16)
                        for i in range(2)]
                sp_l = [self.sb(SP, "sp_l%d" % i, (128, D), BF16)
                        for i in range(2)]
                il_mm1 = make_split_emitters(sp_a, sp_b, sp_h, sp_l)
            SM = contextlib.ExitStack()
            NSLAB = 4
            wsl = [self.sb(SM, "w1s%d%s" % (i, tg), (128, ND * 512), FP8)
                   for i in range(NSLAB)]
            gb_ = [self.sb(SM, "g%d%s" % (i, tg), (128, 512), F32) for i in range(3)]
            b1B = [self.sb(SM, "b1B%d%s" % (i, tg), (128, 512), F32) for i in range(2)]
            s_w1q3 = s_w1q.rearrange("(c p) f -> p c f", p=128)
            v_slab_ok = {}

            slab_engs = (("sync", 0, 6), ("scalar", 6, 11), ("gpsimd", 11, 16))

            def slab_load(fs):
                wv = wsl[fs % NSLAB][:].rearrange("p (c w2) -> p c w2", w2=512)
                for eng, c0, c1 in slab_engs:
                    if fs >= NSLAB:
                        self.w(eng, cT, v_slab_ok[fs - NSLAB])
                    self.dma(eng, wv[:, c0:c1, :],
                             s_w1q3[:, c0:c1, 512 * fs:512 * (fs + 1)],
                             "Ls%s%d%s" % (tg, fs % NSLAB, eng[0]))

            def prep_slab(fs):
                wb = wsl[fs % NSLAB]
                nms = "Ls%s%d" % (tg, fs % NSLAB)
                bcast_row(b1r[fs:fs + 1, :], 6, fs % 2)
                if fs + NSLAB - 1 < NFS:
                    slab_load(fs + NSLAB - 1)
                self.w("vector", cT, cT.n)
                self.q("vector", "tensor_copy", out=b1B[fs % 2][:], in_=bank(6),
                       incs=[(cV, 1)])
                self.dr("vector")
                self.bank_free[6] = (cV, cV.n)
                return wb, nms, cV.n

            for _fs0 in range(NSLAB - 1):
                slab_load(_fs0)

            def mm1_tile(fs, t, wb, nms, v_b1, v_xq):
                bk = (fs * NT + t) % 2
                gsl = (fs * NT + t) % 3
                gb = gb_[gsl]
                if t == 0:
                    for eng, _, _ in slab_engs:
                        self.wdma("tensor", nms + eng[0])
                self.w("tensor", cV, v_xq[t])
                mm_bank_wait(bk)
                for c in range(ND):
                    self.q("tensor", "matmul", bank(bk),
                           xqT[:, c * T + 128 * t:c * T + 128 * (t + 1)],
                           wb[:, 512 * c:512 * (c + 1)],
                           start=(c == 0), stop=(c == ND - 1),
                           incs=[(cT, 1)] if c == ND - 1 else ())
                if t == NT - 1:
                    v_slab_ok[fs] = cT.n
                self.w("vector", cT, cT.n)
                self.w("vector", cV, v_b1)
                self.wdma("vector", "Sg%s%d" % (tg, gsl))
                self.q("vector", "scalar_tensor_tensor", out=gb[:], in0=bank(bk),
                       scalar=st_sv[:, t:t + 1], in1=b1B[fs % 2][:],
                       op0=ALU.mult, op1=ALU.add, incs=[(cV, 1)])
                self.dr("vector")
                self.bank_free[bk] = (cV, cV.n)
                self.w("scalar", cV, cV.n)
                self.q("scalar", "activation", out=gb[:], in_=gb[:], func=AF.Gelu,
                       incs=[(cS, 1)])
                self.w("vector", cS, cS.n)
                self.q("vector", "bn_stats",
                       out=hbn[:, 6 * (t * NFS + fs):6 * (t * NFS + fs) + 6],
                       in_=gb[:], incs=[(cV, 1)])
                self.dr("vector")
                self.w("scalar", cV, cV.n)
                self.dma("scalar",
                         s_g[128 * t:128 * (t + 1), 512 * fs:512 * (fs + 1)],
                         gb[:], "Sg%s%d" % (tg, gsl))

            slab0 = prep_slab(0)
            v_xq = []
            v_dstep = []
            for t in range(NT):
                xb = xb_[t % 2]
                nmx = "Lx%s%d" % (tg, t % 2)
                if t >= 2:
                    self.w("sync", cV, v_dstep[t - 2])
                self.dma("sync", xb[:], x_src[128 * t:128 * (t + 1), :], nmx)
                self.wdma("vector", nmx)
                ln_stats_tile(xb, st_mu[:, t:t + 1], st_rstd[:, t:t + 1])
                self.q("vector", "tensor_scalar", out=db[:], in0=xb[:],
                       scalar1=st_mu[:, t:t + 1], scalar2=0.0,
                       op0=ALU.subtract, op1=ALU.max, accum_out=st_t[:, 4:5],
                       incs=[(cV, 1)])
                self.dr("vector")
                v_dstep.append(cV.n)
                self.q("vector", "tensor_scalar", out=ub[:], in0=db[:],
                       scalar1=-1.0, scalar2=0.0,
                       op0=ALU.mult, op1=ALU.max, accum_out=st_t[:, 9:10])
                self.dr("vector")
                self.q("vector", "tensor_tensor", out=st_t[:, 4:5], in0=st_t[:, 4:5],
                       in1=st_t[:, 9:10], op=ALU.max)
                self.dr("vector")
                self.q("vector", "tensor_scalar", out=st_m2[:, t:t + 1], in0=st_t[:, 4:5],
                       scalar1=st_rstd[:, t:t + 1], scalar2=EPS,
                       op0=ALU.mult, op1=ALU.max)
                self.dr("vector")
                self.q("vector", "reciprocal", out=st_t[:, 5:6], in_=st_m2[:, t:t + 1])
                self.dr("vector")
                self.q("vector", "tensor_scalar", out=st_t[:, 6:7], in0=st_t[:, 5:6],
                       scalar1=st_rstd[:, t:t + 1], scalar2=127.0,
                       op0=ALU.mult, op1=ALU.mult)
                self.dr("vector")
                self.q("vector", "tensor_scalar", out=ub[:], in0=db[:],
                       scalar1=st_t[:, 6:7], scalar2=512.0, op0=ALU.mult, op1=ALU.add,
                       incs=[(cV, 1)])
                self.dr("vector")
                if cid == 0:
                    round_chain(ub[:], r0b[:], iqb[:])
                else:
                    round_wide(ub[:], r0b[:], db[:], xb[:], iqb[:])
                v_iq = self.cV.n
                v_dstep[-1] = v_iq
                self.w("vector", cV, v_sc1)
                self.q("vector", "tensor_scalar", out=st_sv[:, t:t + 1],
                       in0=st_m2[:, t:t + 1], scalar1=st_sc[:, 0:1],
                       scalar2=k127i[:, 0:1], op0=ALU.mult, op1=ALU.mult)
                self.dr("vector")
                for g2 in range(2):
                    self.w("tensor", cV, v_iq)
                    if ("psH", 0) in self.bank_free:
                        c_, v_ = self.bank_free[("psH", 0)]
                        self.w("tensor", c_, v_)
                    for j in range(8):
                        c = g2 * 8 + j
                        self.q("tensor", "transpose", psH[:, 128 * j:128 * (j + 1)],
                               iqb[:, 128 * c:128 * (c + 1)], k_idb[:],
                               incs=[(cT, 1)] if j == 7 else ())
                    self.w("vector", cT, cT.n)
                    self.q("vector", "tensor_copy",
                           out=xqT3[:, g2 * 8:(g2 + 1) * 8, 128 * t:128 * (t + 1)],
                           in_=psH[:].rearrange("p (j t2) -> p j t2", j=8),
                           incs=[(cV, 1)])
                    self.dr("vector")
                    self.bank_free[("psH", 0)] = (cV, cV.n)
                v_xq.append(cV.n)
                mm1_tile(0, t, slab0[0], slab0[1], slab0[2], v_xq)

            mm1_queue = list(il_mm1)
            for fs in range(1, NFS):
                wb, nms, v_b1 = prep_slab(fs)
                for t in range(NT):
                    mm1_tile(fs, t, wb, nms, v_b1, v_xq)
                    if t % 3 == 2 and mm1_queue:
                        mm1_queue.pop(0)()
            while mm1_queue:
                mm1_queue.pop(0)()
            SM.close()
            SP.close()
            SA.close()
            SX.close()
            self.barrier()

            W2Q = contextlib.ExitStack()
            il_pb = ()
            if prep:
                w2_a = self.sb(W2Q, "w2_a", (128, D), F32)
                w2_b = self.sb(W2Q, "w2_b", (128, D), F32)
                w2_r = self.sb(W2Q, "w2_r", (128, D), I16)
                w2_q = self.sb(W2Q, "w2_q", (128, D), FP8)
                w2_s = self.sb(W2Q, "w2_s", (128, NFC * (D // 512)), F32)
                il_pb = make_w2q_emitters(w2_a, w2_b, w2_r, w2_q, w2_s)
            SB = contextlib.ExitStack()
            gt = self.sb(SB, "pb_g%s" % tg, (128, FF), F32)
            gamBF = self.sb(SB, "pb_gB%s" % tg, (128, FF), F32)
            betBF = self.sb(SB, "pb_bB%s" % tg, (128, FF), F32)
            r0h = self.sb(SB, "pb_r%s" % tg, (128, FF), I16)
            iqh = self.sb(SB, "pb_q%s" % tg, (128, FF), BF16)
            pbw = self.sb(SB, "pb_w%s" % tg, (128, FF // 4), F32)
            pbc = self.sb(SB, "pb_c%s" % tg, (128, FF // 4), F32)
            hqc = self.sb(SB, "pb_h%s" % tg, (128, NFC * 128), BF16)
            hqc3 = hqc[:].rearrange("p (c t2) -> p c t2", c=NFC)
            for fs in range(NFS):
                bcast_row(gFr[fs:fs + 1, :], 0, 0)
                self.w("vector", cT, cT.n)
                self.q("vector", "tensor_copy", out=gamBF[:, 512 * fs:512 * (fs + 1)],
                       in_=bank(0), incs=[(cV, 1)])
                self.dr("vector")
                self.bank_free[0] = (cV, cV.n)
                bcast_row(bFr[fs:fs + 1, :], 1, 1)
                self.w("vector", cT, cT.n)
                self.q("vector", "tensor_copy", out=betBF[:, 512 * fs:512 * (fs + 1)],
                       in_=bank(1), incs=[(cV, 1)])
                self.dr("vector")
                self.bank_free[1] = (cV, cV.n)
            pb_queue = list(il_pb)

            def pb_idle():
                if pb_queue:
                    pb_queue.pop(0)()
            v_iqh_prev = 0
            for t in range(NT):
                if t >= 1:
                    self.w("sync", cV, v_iqh_prev)
                self.dma("sync", gt[:], s_g[128 * t:128 * (t + 1), :], "Lg" + tg)
                self.q("vector", "bn_aggr", out=st_t[:, 0:2],
                       in_=hbn[:, 6 * (t * NFS):6 * (t * NFS) + 6 * NFS])
                self.dr("vector")
                self.q("vector", "tensor_scalar", out=st_t[:, 2:3], in0=st_t[:, 1:2],
                       scalar1=EPS, scalar2=None, op0=ALU.add)
                self.dr("vector")
                self.q("vector", "reciprocal", out=st_t[:, 3:4], in_=st_t[:, 2:3],
                       incs=[(cV, 1)])
                self.dr("vector")
                self.w("scalar", cV, cV.n)
                self.q("scalar", "activation", out=st_t[:, 10:11], in_=st_t[:, 3:4],
                       func=AF.Sqrt, incs=[(cS, 1)])
                self.w("vector", cS, cS.n)
                self.q("vector", "tensor_scalar", out=st_t[:, 11:12], in0=st_t[:, 10:11],
                       scalar1=st_t[:, 10:11], scalar2=None, op0=ALU.mult)
                self.dr("vector")
                self.q("vector", "tensor_scalar", out=st_t[:, 11:12], in0=st_t[:, 11:12],
                       scalar1=st_t[:, 2:3], scalar2=None, op0=ALU.mult)
                self.dr("vector")
                self.q("vector", "tensor_scalar", out=st_t[:, 11:12], in0=st_t[:, 11:12],
                       scalar1=-0.5, scalar2=1.5, op0=ALU.mult, op1=ALU.add)
                self.dr("vector")
                self.q("vector", "tensor_scalar", out=st_t[:, 4:5], in0=st_t[:, 11:12],
                       scalar1=st_t[:, 10:11], scalar2=None, op0=ALU.mult)
                self.dr("vector")
                self.wdma("vector", "Lg" + tg)
                self.q("vector", "tensor_scalar", out=gt[:], in0=gt[:],
                       scalar1=st_t[:, 0:1], scalar2=st_t[:, 4:5],
                       op0=ALU.subtract, op1=ALU.mult)
                self.dr("vector")
                self.q("vector", "scalar_tensor_tensor", out=gt[:], in0=gt[:],
                       scalar=1.0, in1=gamBF[:], op0=ALU.mult, op1=ALU.mult)
                self.dr("vector")
                self.q("vector", "tensor_tensor", out=gt[:], in0=gt[:], in1=betBF[:],
                       op=ALU.add)
                self.dr("vector")
                self.q("vector", "tensor_scalar", out=iqh[:], in0=gt[:],
                       scalar1=1.0, scalar2=0.0, op0=ALU.mult, op1=ALU.max,
                       accum_out=st_t[:, 5:6])
                self.dr("vector")
                self.q("vector", "tensor_scalar", out=iqh[:], in0=gt[:],
                       scalar1=-1.0, scalar2=0.0, op0=ALU.mult, op1=ALU.max,
                       accum_out=st_t[:, 9:10])
                self.dr("vector")
                self.q("vector", "tensor_tensor", out=st_t[:, 5:6], in0=st_t[:, 5:6],
                       in1=st_t[:, 9:10], op=ALU.max)
                self.dr("vector")
                self.q("vector", "tensor_scalar", out=st_t[:, 6:7], in0=st_t[:, 5:6],
                       scalar1=EPS, scalar2=None, op0=ALU.max)
                self.dr("vector")
                self.q("vector", "reciprocal", out=st_t[:, 7:8], in_=st_t[:, 6:7])
                self.dr("vector")
                self.q("vector", "tensor_scalar", out=st_t[:, 8:9], in0=st_t[:, 7:8],
                       scalar1=127.0, scalar2=None, op0=ALU.mult)
                self.dr("vector")
                self.q("vector", "tensor_copy", out=st_sv2[:, t:t + 1],
                       in_=st_t[:, 6:7])
                self.dr("vector")
                self.q("vector", "tensor_scalar", out=gt[:], in0=gt[:],
                       scalar1=st_t[:, 8:9], scalar2=512.0, op0=ALU.mult, op1=ALU.add,
                       incs=[(cV, 1)])
                self.dr("vector")
                v_iqh = round_wide(gt[:], r0h[:], pbw[:], pbc[:], iqh[:],
                                   nhalf=4, idle_cb=pb_idle)
                v_iqh_prev = v_iqh
                for g8 in range(8):
                    self.w("tensor", cV, v_iqh)
                    if ("psH", 0) in self.bank_free:
                        c_, v_ = self.bank_free[("psH", 0)]
                        self.w("tensor", c_, v_)
                    for j in range(8):
                        c = g8 * 8 + j
                        self.q("tensor", "transpose", psH[:, 128 * j:128 * (j + 1)],
                               iqh[:, 128 * c:128 * (c + 1)], k_idb[:],
                               incs=[(cT, 1)] if j == 7 else ())
                    self.w("scalar", cT, cT.n)
                    if g8 == 0:
                        self.wdma("scalar", "Shq" + tg)
                    self.q("scalar", "activation",
                           out=hqc3[:, g8 * 8:(g8 + 1) * 8, :],
                           in_=psH[:].rearrange("p (j t2) -> p j t2", j=8),
                           func=AF.Copy, incs=[(cS, 1)])
                    self.bank_free[("psH", 0)] = (cS, cS.n)
                self.dr("scalar")
                self.dma("scalar", s_hq[t], hqc[:], "Shq" + tg)
            while pb_queue:
                pb_queue.pop(0)()
            self.q("vector", "tensor_scalar", out=st_sv2[:, 0:NT],
                   in0=st_sv2[:, 0:NT], scalar1=st_sc[:, 2:3],
                   scalar2=k127i[:, 0:1], op0=ALU.mult, op1=ALU.mult)
            self.dr("vector")
            self.q("vector", "tensor_scalar", out=st_sv2[:, 0:NT],
                   in0=st_sv2[:, 0:NT], scalar1=0.5, scalar2=None,
                   op0=ALU.mult)
            self.dr("vector")
            SB.close()
            W2Q.close()
            HB.close()
            self.barrier()

            SC = contextlib.ExitStack()
            w2res = self.sb(SC, "w2r%s" % tg, (128, NFC * D), FP8)
            hq_ = [self.sb(SC, "hq%d%s" % (i, tg), (128, NFC * 128), BF16)
                   for i in range(2)]
            xr = self.sb(SC, "xr%s" % tg, (128, D), F32)
            xo = self.sb(SC, "xo%s" % tg, (128, D), F32)
            self.wdma("sync", "Sww2")
            self.dma("sync", w2res[:], s_w2q[:], "Lw2" + tg)
            v_ep = {}
            for t in range(NT):
                hsl = t % 2
                hq = hq_[hsl]
                nmh = "Lh%s%d" % (tg, hsl)
                if t >= 2:
                    self.w("sync", cV, v_ep[t - 2])
                if t >= 1:
                    self.w("sync", cV, v_ep[t - 1])
                self.dma("sync", hq[:], s_hq[t], nmh)
                self.dma("sync", xr[:], x_src[128 * t:128 * (t + 1), :], "Lxr" + tg)
                self.wdma("vector", "Lxr" + tg)
                self.wdma("vector", "Sxn" + tg)
                self.q("vector", "tensor_tensor", out=xo[:], in0=xr[:], in1=k_b2B[:],
                       op=ALU.add, incs=[(cV, 1)])
                self.dr("vector")
                v_xo = cV.n
                for ds in range(4):
                    bk = ds % 2 + 2
                    mm_bank_wait(bk)
                    self.wdma("tensor", "Lw2" + tg)
                    self.wdma("tensor", nmh)
                    for c in range(NFC):
                        self.q("tensor", "matmul", bank(bk),
                               hq[:, 128 * c:128 * (c + 1)],
                               w2res[:, c * D + 512 * ds:c * D + 512 * (ds + 1)],
                               start=(c == 0), stop=(c == NFC - 1),
                               incs=[(cT, 1)] if c == NFC - 1 else ())
                    self.w("vector", cT, cT.n)
                    self.w("vector", cV, v_xo)
                    self.q("vector", "scalar_tensor_tensor",
                           out=xo[:, 512 * ds:512 * (ds + 1)], in0=bank(bk),
                           scalar=st_sv2[:, t:t + 1],
                           in1=xo[:, 512 * ds:512 * (ds + 1)],
                           op0=ALU.mult, op1=ALU.add, incs=[(cV, 1)])
                    self.dr("vector")
                    self.bank_free[bk] = (cV, cV.n)
                v_ep[t] = cV.n
                self.w("scalar", cV, cV.n)
                self.dma("scalar", x_next_dst[128 * t:128 * (t + 1), :],
                         xo[:], "Sxn" + tg)
            SC.close()
            CALL.close()
            self.barrier()

        def attn(x_src, x_next_dst):
            ACALL = contextlib.ExitStack()
            k_m4 = self.sb(ACALL, "k_m4", (128, 896), BF16)
            self.dma("sync", k_m4[:], mask4[:], "Lm4")
            st_amu = self.sb(ACALL, "st_amu", (128, NT), F32)
            st_ars = self.sb(ACALL, "st_ars", (128, NT), F32)

            SX = contextlib.ExitStack()
            xnT_hi = self.sb(SX, "xnT_hi", (128, ND * T), BF16)
            xnT_lo = self.sb(SX, "xnT_lo", (128, ND * T), BF16)
            xnT_hi3 = xnT_hi[:].rearrange("p (c t2) -> p c t2", c=ND)
            xnT_lo3 = xnT_lo[:].rearrange("p (c t2) -> p c t2", c=ND)
            SA = contextlib.ExitStack()
            xb_ = [self.sb(SA, "aa_x%d" % i, (128, D), F32) for i in range(2)]
            v_xn = []
            for t in range(NT):
                xb = xb_[t % 2]
                nmx = "Lax%d" % (t % 2)
                if t >= 2:
                    self.w("sync", cV, v_xn[t - 2])
                self.dma("sync", xb[:], x_src[128 * t:128 * (t + 1), :], nmx)
                self.wdma("vector", nmx)
                ln_stats_tile(xb, st_amu[:, t:t + 1], st_ars[:, t:t + 1])
                self.q("vector", "tensor_scalar", out=xb[:], in0=xb[:],
                       scalar1=st_amu[:, t:t + 1], scalar2=st_ars[:, t:t + 1],
                       op0=ALU.subtract, op1=ALU.mult, incs=[(cV, 1)])
                self.dr("vector")
                v_xnb = cV.n
                for g4 in range(4):
                    self.w("tensor", cV, v_xnb)
                    mm_bank_wait(0)
                    for j in range(4):
                        c = g4 * 4 + j
                        self.q("tensor", "transpose", psF[:, 128 * j:128 * (j + 1)],
                               xb[:, 128 * c:128 * (c + 1)], k_idf[:],
                               incs=[(cT, 1)] if j == 3 else ())
                    self.w("vector", cT, cT.n)
                    ps4 = bank(0).rearrange("p (j t2) -> p j t2", j=4)
                    self.q("vector", "tensor_copy",
                           out=xnT_hi3[:, g4 * 4:(g4 + 1) * 4, 128 * t:128 * (t + 1)],
                           in_=ps4)
                    self.dr("vector")
                    self.q("vector", "scalar_tensor_tensor",
                           out=xnT_lo3[:, g4 * 4:(g4 + 1) * 4, 128 * t:128 * (t + 1)],
                           in0=xnT_hi3[:, g4 * 4:(g4 + 1) * 4, 128 * t:128 * (t + 1)],
                           scalar=-1.0, in1=ps4, op0=ALU.mult, op1=ALU.add,
                           incs=[(cV, 1)])
                    self.dr("vector")
                    self.bank_free[0] = (cV, cV.n)
                v_xn.append(cV.n)
            SA.close()

            SH = contextlib.ExitStack()
            qT = {hl: self.sb(SH, "a_q%d" % hl, (128, T), BF16) for hl in range(2)}
            kT = {hl: self.sb(SH, "a_k%d" % hl, (128, T), BF16) for hl in range(2)}
            vB = {hl: self.sb(SH, "a_v%d" % hl, (128, T), BF16) for hl in range(2)}
            wsl_ = [(self.sb(SH, "a_wh%d" % i, (128, ND * 128), BF16),
                     self.sb(SH, "a_wl%d" % i, (128, ND * 128), BF16))
                    for i in range(2)]
            yb_ = {nm: self.sb(SH, "a_y" + nm, (128, T), F32)
                   for nm in ("q", "k")}
            yb_["v"] = yb_["q"]
            NSL = T // 512
            syr = self.sb(SH, "a_sy", (128, 512), F32)
            sy2r = self.sb(SH, "a_sy2", (128, 512), F32)
            mur = self.sb(SH, "a_mu", (128, 512), F32)
            rsr = self.sb(SH, "a_rs", (128, 512), F32)
            for _buf in (syr, sy2r, mur, rsr):
                self.q("vector", "memset", _buf[:], 0.0)
                self.dr("vector")
            s2f_ = [self.sb(SH, "a_s2f%d" % i, (128, 512), F32) for i in range(2)]
            s2h_ = [self.sb(SH, "a_s2h%d" % i, (128, 512), BF16) for i in range(2)]
            mu0 = self.sb(SH, "a_mu0", (1, 512), F32)
            rs0 = self.sb(SH, "a_rs0", (1, 512), F32)
            s2l_ = [self.sb(SH, "a_s2l%d" % i, (128, 512), BF16) for i in range(2)]
            zrow = rsr
            if T >= 1024:
                rzB = yb_["k"][:, 0:512]
                ob = yb_["k"][:, 512:1024]
            else:
                rzB = self.sb(SH, "a_rz", (128, 512), F32)[:]
                ob = self.sb(SH, "a_ob", (128, 512), F32)[:]
            obh = s2h_[0]
            obl = s2l_[0]

            slab_used = [0, 0]
            self.wdma("sync", "Sspill0")
            self.wdma("sync", "Sspill1")

            def load_slabs(si, nm, h):
                if h == 0:
                    self.w("sync", cV, v_xn[min(3, NT - 1)])
                self.w("sync", cT, slab_used[si])
                wslh, wsll = wsl_[si]
                for c in range(ND):
                    self.dma("sync", wslh[:, 128 * c:128 * (c + 1)],
                             s_wh[nm][h, 0, c], "Lws%d" % si)
                    self.dma("sync", wsll[:, 128 * c:128 * (c + 1)],
                             s_wh[nm][h, 1, c], "Lws%d" % si)

            def proj_mm(si, nm):
                wslh, wsll = wsl_[si]
                yb = yb_[nm]
                v_y = []
                s_sq = []

                def stat_pair(ts):
                    if ts == 0:
                        mm_bank_wait(5)
                        mm_bank_wait(6)
                    if ts == 3:
                        mm_bank_wait(4)
                        mm_bank_wait(2)
                    sl = slice(512 * ts, 512 * (ts + 1))
                    self.w("tensor", cV, v_y[ts])
                    self.w("tensor", cS, s_sq[ts])
                    oy = (psF[32 * ts:32 * ts + 1, 3072:3584] if ts < 3
                          else psF[0:1, 2048:2560])
                    oy2 = (psF[32 * ts:32 * ts + 1, 2560:3072] if ts < 3
                           else psF[0:1, 1024:1536])
                    self.q("tensor", "matmul", oy, k_1cf[:], yb[:, sl],
                           start=True, stop=True)
                    self.q("tensor", "matmul", oy2,
                           k_1cf[:], s2f_[ts % 2][:], start=True, stop=True,
                           incs=[(cT, 1)] if ts == NSL - 1 else ())

                for ts in range(NSL):
                    if self.proj_off:
                        break
                    bk = ts % 2
                    mm_bank_wait(bk)
                    self.wdma("tensor", "Lws%d" % si)
                    self.w("tensor", cV, v_xn[min(4 * ts + 3, NT - 1)])
                    for c in range(ND):
                        a0 = c * T + 512 * ts
                        self.q("tensor", "matmul", bank(bk),
                               wslh[:, 128 * c:128 * (c + 1)],
                               xnT_hi[:, a0:a0 + 512], start=(c == 0), stop=False)
                        self.q("tensor", "matmul", bank(bk),
                               wsll[:, 128 * c:128 * (c + 1)],
                               xnT_hi[:, a0:a0 + 512], start=False, stop=False)
                        self.q("tensor", "matmul", bank(bk),
                               wslh[:, 128 * c:128 * (c + 1)],
                               xnT_lo[:, a0:a0 + 512], start=False,
                               stop=(c == ND - 1),
                               incs=[(cT, 1)] if c == ND - 1 else ())
                    if ts == NSL - 1:
                        slab_used[si] = cT.n
                    self.w("vector", cT, cT.n)
                    sl = slice(512 * ts, 512 * (ts + 1))
                    if nm == "v":
                        self.q("vector", "tensor_copy", out=yb[:, sl],
                               in_=bank(bk), incs=[(cV, 1)])
                        self.dr("vector")
                    else:
                        self.q("vector", "tensor_scalar", out=yb[:, sl],
                               in0=bank(bk), scalar1=k_gamc[:, 0:1],
                               scalar2=k_betc[:, 0:1], op0=ALU.mult, op1=ALU.add,
                               incs=[(cV, 1)])
                        self.dr("vector")
                    self.bank_free[bk] = (cV, cV.n)
                    v_y.append(cV.n)
                    if nm != "v" and not self.fm_off:
                        self.w("scalar", cV, cV.n)
                        self.q("scalar", "activation",
                               out=s2f_[ts % 2][:],
                               in_=yb[:, sl], func=AF.Square,
                               incs=[(cS, 1)])
                        s_sq.append(cS.n)
                        if ts >= 1:
                            stat_pair(ts - 1)
                if nm != "v" and not self.fm_off:
                    stat_pair(NSL - 1)
                    self.w("vector", cT, cT.n)
                    for ts in range(min(NSL, 3)):
                        self.q("vector", "tensor_copy",
                               out=syr[32 * ts:32 * ts + 1, :],
                               in_=psF[32 * ts:32 * ts + 1, 3072:3584])
                        self.dr("vector")
                        self.q("vector", "tensor_copy",
                               out=sy2r[32 * ts:32 * ts + 1, :],
                               in_=psF[32 * ts:32 * ts + 1, 2560:3072],
                               incs=[(cV, 1)] if (NSL < 4 and ts == NSL - 1)
                               else ())
                        self.dr("vector")
                    if NSL == 4:
                        self.q("vector", "tensor_copy", out=syr[96:97, :],
                               in_=psF[0:1, 2048:2560])
                        self.dr("vector")
                        self.q("vector", "tensor_copy", out=sy2r[96:97, :],
                               in_=psF[0:1, 1024:1536], incs=[(cV, 1)])
                        self.dr("vector")
                        self.bank_free[4] = (cV, cV.n)
                        self.bank_free[2] = (cV, cV.n)
                    self.bank_free[5] = (cV, cV.n)
                    self.bank_free[6] = (cV, cV.n)

            def row_math(nm):
                self.q("vector", "tensor_scalar", out=mur[:, :],
                       in0=syr[:, :], scalar1=1.0 / 128.0, scalar2=None,
                       op0=ALU.mult)
                self.dr("vector")
                self.q("vector", "tensor_scalar", out=rsr[:, :],
                       in0=sy2r[:, :], scalar1=1.0 / 128.0, scalar2=None,
                       op0=ALU.mult)
                self.dr("vector")
                self.q("vector", "tensor_tensor", out=syr[:, :],
                       in0=mur[:, :], in1=mur[:, :], op=ALU.mult)
                self.dr("vector")
                self.q("vector", "scalar_tensor_tensor", out=rsr[:, :],
                       in0=syr[:, :], scalar=-1.0, in1=rsr[:, :],
                       op0=ALU.mult, op1=ALU.add)
                self.dr("vector")
                self.q("vector", "tensor_scalar", out=rsr[:, :],
                       in0=rsr[:, :], scalar1=EPS, scalar2=None, op0=ALU.add)
                self.dr("vector")
                self.q("vector", "tensor_copy", out=syr[:, :], in_=rsr[:, :])
                self.dr("vector")
                self.q("vector", "reciprocal", out=rsr[:, :],
                       in_=rsr[:, :], incs=[(cV, 1)])
                self.dr("vector")
                self.w("scalar", cV, cV.n)
                self.q("scalar", "activation", out=rsr[:, :],
                       in_=rsr[:, :], func=AF.Sqrt, incs=[(cS, 1)])
                self.w("vector", cS, cS.n)
                self.q("vector", "tensor_tensor", out=sy2r[:, :], in0=rsr[:, :],
                       in1=rsr[:, :], op=ALU.mult)
                self.dr("vector")
                self.q("vector", "tensor_tensor", out=sy2r[:, :], in0=sy2r[:, :],
                       in1=syr[:, :], op=ALU.mult)
                self.dr("vector")
                self.q("vector", "tensor_scalar", out=sy2r[:, :], in0=sy2r[:, :],
                       scalar1=-0.5, scalar2=1.5, op0=ALU.mult, op1=ALU.add)
                self.dr("vector")
                self.q("vector", "tensor_tensor", out=rsr[:, :], in0=rsr[:, :],
                       in1=sy2r[:, :], op=ALU.mult)
                self.dr("vector")
                if nm == "q":
                    self.q("vector", "tensor_scalar", out=rsr[:, :],
                           in0=rsr[:, :], scalar1=kRT2[:, 0:1], scalar2=None,
                           op0=ALU.mult)
                    self.dr("vector")
                if NSL == 4:
                    self.q("vector", "tensor_copy", out=mu0[0:1, :],
                           in_=mur[96:97, :])
                    self.dr("vector")
                    self.q("vector", "tensor_copy", out=rs0[0:1, :],
                           in_=rsr[96:97, :])
                    self.dr("vector")
                self.q("vector", "nop", incs=[(cV, 1)])
                return cV.n

            def fm_apply(nm, v_stats):
                dstT = qT if nm == "q" else kT
                yb = yb_[nm]
                for ts in range(NSL):
                    bmu, brs = (2, 3) if ts % 2 == 0 else (4, 5)
                    self.w("tensor", cV, v_stats)
                    mm_bank_wait(bmu)
                    mm_bank_wait(brs)
                    if ts < 3:
                        rhs_mu = mur[32 * ts:32 * ts + 1, :]
                        rhs_rs = rsr[32 * ts:32 * ts + 1, :]
                        ones = k_1all[32 * ts:32 * ts + 1, :]
                    else:
                        rhs_mu = mu0[0:1, :]
                        rhs_rs = rs0[0:1, :]
                        ones = None
                    pe_bcast(bank(bmu), rhs_mu, ones)
                    pe_bcast(bank(brs), rhs_rs, ones)
                    self.w("vector", cT, cT.n)
                    sl = slice(512 * ts, 512 * (ts + 1))
                    self.q("vector", "tensor_tensor", out=yb[:, sl], in0=yb[:, sl],
                           in1=bank(bmu), op=ALU.subtract)
                    self.dr("vector")
                    self.q("vector", "tensor_tensor", out=yb[:, sl], in0=yb[:, sl],
                           in1=bank(brs), op=ALU.mult, incs=[(cV, 1)])
                    self.dr("vector")
                    self.bank_free[bmu] = (cV, cV.n)
                    self.bank_free[brs] = (cV, cV.n)
                    self.q("vector", "tensor_copy", out=dstT[0][:, sl],
                           in_=yb[:, sl])
                    self.dr("vector")
                    self.q("vector", "scalar_tensor_tensor", out=dstT[1][:, sl],
                           in0=dstT[0][:, sl], scalar=-1.0, in1=yb[:, sl],
                           op0=ALU.mult, op1=ALU.add, incs=[(cV, 1)])
                    self.dr("vector")

            def v_transposes():
                yb = yb_["v"]
                v_done = {}
                for t in range(NT):
                    r = t % 4
                    reg = psF[:, 1024 + 512 * r:1024 + 512 * r + 128]
                    if t >= 4:
                        self.w("tensor", cV, v_done[t - 4])
                    else:
                        mm_bank_wait(2 + r)
                    self.q("tensor", "transpose", reg,
                           yb[:, 128 * t:128 * (t + 1)], k_idf[:],
                           incs=[(cT, 1)])
                    self.w("vector", cT, cT.n)
                    self.q("vector", "tensor_copy",
                           out=vB[0][:, 128 * t:128 * (t + 1)], in_=reg)
                    self.dr("vector")
                    self.q("vector", "scalar_tensor_tensor",
                           out=vB[1][:, 128 * t:128 * (t + 1)],
                           in0=vB[0][:, 128 * t:128 * (t + 1)], scalar=-1.0,
                           in1=reg, op0=ALU.mult, op1=ALU.add,
                           incs=[(cV, 1)])
                    self.dr("vector")
                    v_done[t] = cV.n
                for r in range(4):
                    last_t = (NT - 4) + r if NT >= 4 else r
                    if last_t < NT:
                        self.bank_free[2 + r] = (cV, v_done[last_t])

            self._fl_split = {0: 0, 1: 0}
            self._fl_avread = {0: 0, 1: 0}
            self.wdma("vector", "Lm4")
            for h in range(self.n_heads):
                load_slabs(0, "q", h)
                proj_mm(0, "q")
                load_slabs(1, "k", h)
                v_rm_q = 0 if self.fm_off else row_math("q")
                proj_mm(1, "k")
                if not self.fm_off:
                    fm_apply("q", v_rm_q)
                load_slabs(0, "v", h)
                v_rm_k = 0 if self.fm_off else row_math("k")
                proj_mm(0, "v")
                if not self.fm_off:
                    fm_apply("k", v_rm_k)
                v_head_ready = cV.n
                if not self.vt_off:
                    v_transposes()

                for qs in range(min(NQS, self.nqs_cap)):
                    nkb = 4 * (qs + 1)
                    q0 = 512 * qs
                    t_qk = {}
                    t_av = {}
                    s_sq = {}
                    v_split = {}

                    def emit_qk(kb):
                        bkS = 4 + (kb % 2)
                        mm_bank_wait(bkS)
                        if qs == 0 and kb == 0:
                            self.w("tensor", cV, v_head_ready)
                        self.q("tensor", "matmul", bank(bkS),
                               kT[0][:, 128 * kb:128 * (kb + 1)],
                               qT[0][:, q0:q0 + 512], start=True, stop=False)
                        self.q("tensor", "matmul", bank(bkS),
                               kT[0][:, 128 * kb:128 * (kb + 1)],
                               qT[1][:, q0:q0 + 512], start=False, stop=False)
                        self.q("tensor", "matmul", bank(bkS),
                               kT[1][:, 128 * kb:128 * (kb + 1)],
                               qT[0][:, q0:q0 + 512], start=False, stop=True,
                               incs=[(cT, 1)])
                        t_qk[kb] = cT.n

                    def emit_sq_split(kb):
                        bkS = 4 + (kb % 2)
                        pi = kb % 2
                        self.w("scalar", cT, t_qk[kb])
                        self.w("scalar", cV, self._fl_split[pi])
                        self.q("scalar", "activation", out=s2f_[pi][:],
                               in_=bank(bkS), func=AF.Square, incs=[(cS, 1)])
                        s_sq[kb] = cS.n
                        self.bank_free[bkS] = (cS, s_sq[kb])
                        self.w("vector", cS, s_sq[kb])
                        self.w("vector", cT, self._fl_avread[pi])
                        if pi == 0:
                            self.wdma("vector", "So")
                        if kb >= 4 * qs:
                            m = kb - 4 * qs
                            self.q("vector", "tensor_tensor", out=s2f_[pi][:],
                                   in0=s2f_[pi][:],
                                   in1=k_m4[:, 384 - 128 * m:896 - 128 * m],
                                   op=ALU.mult)
                            self.dr("vector")
                        self.q("vector", "tensor_copy", out=s2h_[pi][:],
                               in_=s2f_[pi][:])
                        self.dr("vector")
                        self.q("vector", "scalar_tensor_tensor", out=s2l_[pi][:],
                               in0=s2h_[pi][:], scalar=-1.0, in1=s2f_[pi][:],
                               op0=ALU.mult, op1=ALU.add, incs=[(cV, 1)])
                        self.dr("vector")
                        v_split[kb] = cV.n
                        self._fl_split[pi] = cV.n

                    def emit_av(kb):
                        pi = kb % 2
                        self.w("tensor", cV, v_split[kb])
                        if kb == 0:
                            mm_bank_wait(6)
                            mm_bank_wait(3)
                        self.q("tensor", "matmul", bank(6),
                               vB[0][:, 128 * kb:128 * (kb + 1)], s2h_[pi][:],
                               start=(kb == 0), stop=False)
                        self.q("tensor", "matmul", bank(6),
                               vB[1][:, 128 * kb:128 * (kb + 1)], s2h_[pi][:],
                               start=False, stop=False)
                        self.q("tensor", "matmul", bank(6),
                               vB[0][:, 128 * kb:128 * (kb + 1)], s2l_[pi][:],
                               start=False, stop=(kb == nkb - 1))
                        self.q("tensor", "matmul", bank(3)[0:1, :],
                               k_1cb[:], s2h_[pi][:], start=(kb == 0), stop=False)
                        self.q("tensor", "matmul", bank(3)[0:1, :],
                               k_1cb[:], s2l_[pi][:], start=False,
                               stop=(kb == nkb - 1), incs=[(cT, 1)])
                        t_av[kb] = cT.n
                        self._fl_avread[pi] = cT.n

                    emit_qk(0)
                    emit_sq_split(0)
                    for kb in range(1, nkb):
                        emit_qk(kb)
                        emit_sq_split(kb)
                        emit_av(kb - 1)
                    emit_av(nkb - 1)
                    tv_av = t_av[nkb - 1]
                    self.w("vector", cT, tv_av)
                    self.q("vector", "tensor_scalar", out=zrow[0:1, :],
                           in0=bank(3)[0:1, :], scalar1=EPS, scalar2=None, op0=ALU.add)
                    self.dr("vector")
                    self.q("vector", "reciprocal", out=zrow[0:1, :], in_=zrow[0:1, :],
                           incs=[(cV, 1)])
                    self.dr("vector")
                    self.w("tensor", cV, cV.n)
                    mm_bank_wait(2)
                    pe_bcast(bank(2), zrow[0:1, :])
                    self.w("vector", cT, cT.n)
                    self.q("vector", "tensor_copy", out=rzB, in_=bank(2),
                           incs=[(cV, 1)])
                    self.dr("vector")
                    self.bank_free[2] = (cV, cV.n)
                    self.q("vector", "tensor_tensor", out=ob, in0=bank(6),
                           in1=rzB, op=ALU.mult, incs=[(cV, 1)])
                    self.dr("vector")
                    self.bank_free[6] = (cV, cV.n)
                    self.bank_free[3] = (cV, cV.n)
                    self.wdma("vector", "So")
                    self.q("vector", "tensor_copy", out=obh[:], in_=ob)
                    self.dr("vector")
                    self.q("vector", "scalar_tensor_tensor", out=obl[:],
                           in0=obh[:], scalar=-1.0, in1=ob,
                           op0=ALU.mult, op1=ALU.add, incs=[(cV, 1)])
                    self.dr("vector")
                    self.w("scalar", cV, cV.n)
                    for tc in range(4):
                        t = 4 * qs + tc
                        self.dma("scalar",
                                 s_o[t][:, (2 * h) * 128:(2 * h) * 128 + 128],
                                 obh[:, 128 * tc:128 * (tc + 1)], "So")
                        self.dma("scalar",
                                 s_o[t][:, (2 * h + 1) * 128:(2 * h + 1) * 128 + 128],
                                 obl[:, 128 * tc:128 * (tc + 1)], "So")
            SH.close()
            SX.close()
            self.barrier()

            SW = contextlib.ExitStack()
            wo_ = [self.sb(SW, "wo_res%d" % i, (128, 2 * ND * 512), BF16)
                   for i in range(2)]
            ob_ = [self.sb(SW, "wo_o%d" % i, (128, H * 2 * 128), BF16)
                   for i in range(4)]
            xr2_ = [self.sb(SW, "wo_x%d" % i, (128, 512), F32) for i in range(2)]
            xo2_ = [self.sb(SW, "wo_y%d" % i, (128, 512), F32) for i in range(2)]
            self.wdma("sync", "Sspill0")
            self.wdma("sync", "Sspill1")
            self.wdma("sync", "So")
            v_ep = {}
            wo_used = [0, 0]

            def load_wo(ds):
                self.w("sync", cT, wo_used[ds % 2])
                wos = wo_[ds % 2]
                for hl in range(2):
                    for c in range(ND):
                        off = (hl * ND + c) * 512
                        self.dma("sync", wos[:, off:off + 512], s_woh[ds, hl, c],
                                 "Lwo%d" % (ds % 2))

            load_wo(0)
            for ds in range(4):
                wos = wo_[ds % 2]
                if ds + 1 < 4:
                    load_wo(ds + 1)
                for t in range(NT):
                    obt = ob_[t % 4]
                    nmo = "Lo%d" % (t % 4)
                    if t >= 2 or ds > 0:
                        pk = (ds, t - 2) if t >= 2 else (ds - 1, NT - 2 + t)
                        self.w("sync", cV, v_ep[pk])
                    self.dma("sync", obt[:], s_o[t], nmo)
                    self.dma("sync", xr2_[t % 2][:],
                             x_src[128 * t:128 * (t + 1),
                                   512 * ds:512 * (ds + 1)], "Lxr2%d" % (t % 2))
                    self.wdma("tensor", nmo)
                    self.wdma("tensor", "Lwo%d" % (ds % 2))
                    bk = t % 2
                    mm_bank_wait(bk)
                    for c in range(ND):
                        ohi = obt[:, (2 * c) * 128:(2 * c) * 128 + 128]
                        olo = obt[:, (2 * c + 1) * 128:(2 * c + 1) * 128 + 128]
                        whi = wos[:, (0 * ND + c) * 512:(0 * ND + c) * 512 + 512]
                        wlo = wos[:, (1 * ND + c) * 512:(1 * ND + c) * 512 + 512]
                        self.q("tensor", "matmul", bank(bk), ohi, whi,
                               start=(c == 0), stop=False)
                        self.q("tensor", "matmul", bank(bk), ohi, wlo,
                               start=False, stop=False)
                        self.q("tensor", "matmul", bank(bk), olo, whi,
                               start=False, stop=(c == ND - 1),
                               incs=[(cT, 1)] if c == ND - 1 else ())
                    if t == NT - 1:
                        wo_used[ds % 2] = cT.n
                    self.w("vector", cT, cT.n)
                    self.wdma("vector", "Lxr2%d" % (t % 2))
                    self.wdma("vector", "Sx2%d" % (t % 2))
                    self.q("vector", "tensor_tensor", out=xo2_[t % 2][:],
                           in0=bank(bk), in1=xr2_[t % 2][:], op=ALU.add,
                           incs=[(cV, 1)])
                    self.dr("vector")
                    self.bank_free[bk] = (cV, cV.n)
                    v_ep[(ds, t)] = cV.n
                    self.w("scalar", cV, cV.n)
                    self.dma("scalar",
                             x_next_dst[128 * t:128 * (t + 1), 512 * ds:512 * (ds + 1)],
                             xo2_[t % 2][:], "Sx2%d" % (t % 2))
            SW.close()
            ACALL.close()
            self.barrier()

        if "ffn1" in self.parts:
            ffn(x_in, s_x1, 0)
            cur = s_x1
        else:
            cur = x_in
        if "attn" in self.parts:
            attn(cur, s_x2)
            cur = s_x2
        if "ffn2" in self.parts:
            ffn(cur, y_out, 1)
        else:
            SCP = contextlib.ExitStack()
            tb = self.sb(SCP, "cp_t", (128, D), F32)
            for t in range(NT):
                self.dma("sync", tb[:], cur[128 * t:128 * (t + 1), :], "cp")
                self.wdma("sync", "cp")
                self.dma("sync", y_out[128 * t:128 * (t + 1), :], tb[:], "cp")
                self.wdma("sync", "cp")
            SCP.close()
        for nm2 in list(self.dsems):
            self.wdma("sync", nm2)

        with self.nc.Block() as block:
            @block.sync
            def _(e):
                self.replay("sync", e)

            @block.tensor
            def _(e):
                self.replay("tensor", e)

            @block.vector
            def _(e):
                self.replay("vector", e)

            @block.scalar
            def _(e):
                self.replay("scalar", e)

            @block.gpsimd
            def _(e):
                self.replay("gpsimd", e)
        self._P.close()
        self._sem_stack.close()
        return self.nc

    def replay(self, ename, e):
        for idx, item in enumerate(self.plan[ename]):
            k = item[0]
            if k == "call":
                _, fname, args, kw, incs = item
                try:
                    inst = getattr(e, fname)(*args, **kw)
                except Exception:
                    import sys
                    print("REPLAY FAIL %s[%d] %s args=%r kw=%r" %
                          (ename, idx, fname, args, kw), file=sys.stderr)
                    raise
                for ct, amt in incs:
                    inst = inst.then_inc(ct.h, amt)
            elif k == "wait":
                e.wait_ge(item[1].h, item[2])
            else:
                e.drain()


def host_inputs(inp, T):
    c = np.ascontiguousarray
    f32 = np.float32
    bf = ml_dtypes.bfloat16
    out = {
        "w1t": c(np.asarray(inp["L0_ffn_w1"], f32).T),
        "w2t": c(np.asarray(inp["L0_ffn_w2"], f32).T),
        "wqt": c(np.asarray(inp["L0_attn_Wq"], f32).T),
        "wkt": c(np.asarray(inp["L0_attn_Wk"], f32).T),
        "wvt": c(np.asarray(inp["L0_attn_Wv"], f32).T),
        "wot": c(np.asarray(inp["L0_attn_Wo"], f32).T),
        "b1r": c(np.asarray(inp["L0_ffn_b1"], f32).reshape(16, 512)),
        "b2r": c(np.asarray(inp["L0_ffn_b2"], f32).reshape(4, 512)),
        "gFr": c(np.asarray(inp["L0_ffn_gamma"], f32).reshape(16, 512)),
        "bFr": c(np.asarray(inp["L0_ffn_beta"], f32).reshape(16, 512)),
        "gamc": c(np.asarray(inp["L0_attn_gamma"], f32).reshape(128, 1)),
        "betc": c(np.asarray(inp["L0_attn_beta"], f32).reshape(128, 1)),
        "identf": np.eye(128, dtype=f32),
        "identb": np.eye(128, dtype=bf),
        "onesf": np.ones((1, 128), dtype=f32),
        "onescf": np.ones((128, 1), dtype=f32),
        "onescb": np.ones((128, 1), dtype=bf),
    }
    cc = np.arange(896)[None, :]
    pp = np.arange(128)[:, None]
    out["mask4"] = (cc >= pp + 384).astype(bf)
    return out


def kernel(**inputs):
    T = 2048
    x = np.ascontiguousarray(np.asarray(inputs["L0_attn_hidden_states"],
                                        dtype=np.float32))
    B = x.shape[0]
    shared = host_inputs(inputs, T)
    mk = MK(T, ("ffn1", "attn", "ffn2"))
    nc = mk.build()
    in_maps = []
    for b in range(B):
        m = dict(shared)
        m["x"] = np.ascontiguousarray(x[b])
        in_maps.append(m)
    res = run_bass_kernel_spmd(nc, in_maps, core_ids=list(range(B)))
    return np.stack([np.asarray(r["y"], dtype=np.float32) for r in res.results],
                    axis=0)
